# version 01
# baseline (speedup 1.0000x reference)
import numpy as np
import ml_dtypes
import concourse.bass as bass
import concourse.mybir as mybir
from concourse.bass_utils import run_bass_kernel_spmd

F32 = mybir.dt.float32
BF16 = mybir.dt.bfloat16
AF = mybir.ActivationFunctionType
ALU = mybir.AluOpType
AX = mybir.AxisListType

NCORES = 8
S = 2048
D = 1024
NG = 4
GW = 512
NDT = 8
H = 16
HD = 64
FUSED = False
A1_CH0 = 21
A1_NCH = 32
A1_EPI = True
A1_HALVES = 2
A1_SUB = 3
A1_DBG = False
A1_DBG_CH = 0
A1_SUB2 = 9
A1_BI = 4

_CONSTS = None


def make_consts():
    global _CONSTS
    if _CONSTS is not None:
        return _CONSTS
    c = {}
    c["id128"] = np.eye(128, dtype=np.float32)
    c["onesf"] = np.ones((128, 128), dtype=np.float32)
    c["negonesf"] = -np.ones((1, 128), dtype=np.float32)
    inv = 1.0 / 10000.0 ** (np.arange(0, HD, 2, dtype=np.float64) / HD)
    pos = np.arange(S, dtype=np.float64)
    ang = pos[:, None] * inv[None, :]
    cosv = np.cos(ang).T
    sinv = np.sin(ang).T
    cosT = np.zeros((128, S), np.float32)
    ssinT = np.zeros((128, S), np.float32)
    for r in range(128):
        cosT[r] = cosv[r % 32]
        ssinT[r] = (-sinv if (r % 64) < 32 else sinv)[r % 32]
    c["cosT"] = cosT
    c["ssinT"] = ssinT
    perm = np.zeros((128, 128), np.float32)
    for r in range(128):
        perm[(r // 64) * 64 + ((r % 64) + 32) % 64, r] = 1.0
    c["perm"] = perm
    caus = np.zeros((4, 128, GW), np.float32)
    for j in range(4):
        rk = np.arange(128)[:, None]
        ct = np.arange(GW)[None, :]
        caus[j] = np.where(128 * j + rk > ct, -1e9, 0.0)
    c["caus"] = np.concatenate([caus[j] for j in range(4)], axis=1)
    c["iota8"] = np.broadcast_to(np.arange(8, dtype=np.float32), (128, 8)).copy()
    c["crev8"] = 8.0 - c["iota8"]
    c["iota32"] = np.broadcast_to(np.arange(32, dtype=np.float32), (128, 32)).copy()
    c["crev32"] = 32.0 - c["iota32"]
    j0 = np.arange(128)[:, None]
    t0 = np.arange(64)[None, :]
    c["mYq"] = (j0 <= 2 * t0 + 1).astype(np.float32)
    mZ = (j0 >= 2 * t0 + 2).astype(np.float32)
    c["mZY"] = np.concatenate([c["mYq"], mZ, np.ones((128, 64), np.float32)], 1)
    c["mSU"] = (np.arange(128)[None, :] > np.arange(128)[:, None]).astype(np.float32)
    c["mSL"] = c["mSU"].T.copy()
    tri = (np.arange(64)[:, None] <= np.arange(64)[None, :]).astype(np.float32)
    c["mTri"] = np.concatenate([tri, tri], 0)
    _CONSTS = c
    return c


def _nm(x):
    t = getattr(x, "tensor", None)
    return t.name if t is not None else x.name


SEM_ROTATE = 20000


class Em:
    CE = ("tensor", "vector", "scalar", "gpsimd", "sync")
    DQ = ("sync", "gpsimd", "scalar")

    def __init__(self, nc):
        self.nc = nc
        self.q = {e: [] for e in self.CE}
        self.semidx = {}
        self.sems = {}
        self.cnt = {}
        self.active = {}
        self.NDMA = 8
        for e in self.CE:
            self._newsem("S_" + e)
        for e in self.DQ:
            for r in range(self.NDMA):
                self._newsem(f"D_{e}_{r}")
        self.dseq = {e: 0 for e in self.DQ}
        self.known = {e: {} for e in self.CE}
        self.wr = {}
        self.rd = {}

    def _newsem(self, log):
        i = self.active.get(log, -1) + 1
        h = self.nc.alloc_semaphore(f"{log}_{i}")
        self.sems.setdefault(log, []).append(h)
        self.active[log] = i
        self.cnt[log] = 0

    def _norm(self, x):
        if isinstance(x, tuple):
            return (_nm(x[0]), x[1])
        return (_nm(x), None)

    def _emit_waits(self, eng, need):
        kn = self.known[eng]
        out = []
        for key, v in sorted(need.items()):
            if kn.get(key, 0) >= v:
                continue
            log, idx = key
            out.append((self.sems[log][idx], v))
            kn[key] = v
        return out

    def _hazards(self, eng, reads, writes):
        need = {}

        def add(ent):
            s, v = ent
            if eng == "tensor" and s[0] == "S_tensor":
                return
            if need.get(s, -1) < v:
                need[s] = v
        for (t, k) in reads:
            for wk, ent in self.wr.get(t, {}).items():
                if k is None or wk is None or wk == k:
                    add(ent)
        for (t, k) in writes:
            for wk, ent in self.wr.get(t, {}).items():
                if k is None or wk is None or wk == k:
                    add(ent)
            for rk, lst in self.rd.get(t, {}).items():
                if k is None or rk is None or rk == k:
                    for ent in lst:
                        add(ent)
        return need

    def op(self, eng, build, reads=(), writes=(), dma=False):
        reads = [self._norm(r) for r in reads]
        writes = [self._norm(w) for w in writes]
        need = self._hazards(eng, reads, writes)
        if dma:
            r = self.dseq[eng] % self.NDMA
            self.dseq[eng] += 1
            log = f"D_{eng}_{r}"
            inc = 16
            prev = self.cnt[log]
            if prev > 0:
                key = (log, self.active[log])
                if need.get(key, -1) < prev:
                    need[key] = prev
        else:
            log = "S_" + eng
            inc = 1
        waits = self._emit_waits(eng, need)
        if self.cnt[log] + inc > SEM_ROTATE:
            self.barrier()
            waits = self._emit_waits(eng, self._hazards(eng, reads, writes))
            self._newsem(log)
        self.cnt[log] += inc
        ent = ((log, self.active[log]), self.cnt[log])
        self.q[eng].append((waits, build, self.sems[log][self.active[log]], inc))
        for (t, k) in writes:
            if k is None:
                self.wr[t] = {None: ent}
                self.rd.pop(t, None)
            else:
                self.wr.setdefault(t, {})[k] = ent
                if t in self.rd:
                    self.rd[t].pop(k, None)
        for (t, k) in reads:
            self.rd.setdefault(t, {}).setdefault(k, []).append(ent)

    def barrier(self):
        for e in self.CE:
            kn = self.known[e]
            need = []
            for log in self.cnt:
                c = self.cnt[log]
                if c == 0:
                    continue
                key = (log, self.active[log])
                if kn.get(key, 0) >= c:
                    continue
                need.append((self.sems[log][self.active[log]], c))
                kn[key] = c
            if need:
                self.q[e].append((need, None, None, 0))
        self.wr.clear()
        self.rd.clear()

    def finalize(self):
        with self.nc.Block() as blk:
            for eng in self.CE:
                ops = self.q[eng]

                def mk(ops):
                    def body(e):
                        for waits, build, sem, inc in ops:
                            for (sh, v) in waits:
                                e.wait_ge(sh, v)
                            if build is None:
                                continue
                            inst = build(e)
                            if sem is not None:
                                inst.then_inc(sem, inc)
                    return body
                getattr(blk, eng)(mk(ops))

    def dma(self, queue, out, in_, extra_reads=()):
        self.op(queue, lambda e: e.dma_start(out, in_),
                reads=[in_] + list(extra_reads), writes=[out], dma=True)

    def mm(self, out, lhsT, rhs, start, stop, reads=(), writes=(), tile_position=None):
        self.op("tensor",
                lambda e: e.matmul(out, lhsT, rhs, start=start, stop=stop,
                                   tile_position=tile_position),
                reads=[lhsT, rhs] + list(reads), writes=[out] + list(writes))

    def trans(self, out, in_, ident):
        self.op("tensor", lambda e: e.transpose(out, in_, ident),
                reads=[in_, ident], writes=[out])

    def act(self, out, in_, fn, bias=0.0, scale=1.0, reads=(), writes=()):
        br = [bias] if not isinstance(bias, float) else []
        self.op("scalar", lambda e: e.activation(out, in_, fn, bias=bias, scale=scale),
                reads=[in_] + br + list(reads), writes=[out] + list(writes))

    def tt(self, out, a, b, alu, eng="vector", reads=(), writes=()):
        self.op(eng, lambda e: e.tensor_tensor(out, a, b, alu),
                reads=[a, b] + list(reads), writes=[out] + list(writes))

    def ts(self, out, a, s1, op1, s2=None, op2=ALU.bypass, eng="vector"):
        sr = [s1] if not isinstance(s1, float) else []
        if s2 is not None and not isinstance(s2, float):
            sr.append(s2)
        self.op(eng, lambda e: e.tensor_scalar(out, a, s1, s2, op1, op2),
                reads=[a] + sr, writes=[out])

    def stt(self, out, a, s, b, op0, op1, eng="vector"):
        sr = [s] if not isinstance(s, float) else []
        self.op(eng, lambda e: e.scalar_tensor_tensor(out, a, s, b, op0, op1),
                reads=[a, b] + sr, writes=[out])

    def copy(self, out, in_, eng="vector"):
        if eng == "scalar":
            self.op(eng, lambda e: e.copy(out, in_), reads=[in_], writes=[out])
        else:
            self.op(eng, lambda e: e.tensor_copy(out, in_), reads=[in_], writes=[out])

    def memset(self, t, v, eng="vector"):
        self.op(eng, lambda e: e.memset(t, v), reads=(), writes=[t])

    def red(self, out, in_, alu, negate=False, eng="vector"):
        self.op(eng, lambda e: e.tensor_reduce(out, in_, AX.X, alu, negate=negate),
                reads=[in_], writes=[out])

    def scan(self, out, d0, d1, init, op0, op1):
        self.op("vector", lambda e: e.tensor_tensor_scan(out, d0, d1, init, op0, op1),
                reads=[d0, d1], writes=[out])

    def recip(self, out, in_):
        self.op("vector", lambda e: e.reciprocal(out, in_), reads=[in_], writes=[out])


def ap3(t, part_off, npart, col_off, dims):
    pitch = t.shape[-1]
    return bass.AP(t, part_off * pitch + col_off, [[pitch, npart]] + list(dims))


class Prog:
    def __init__(self, sections, dump=()):
        self.sections = tuple(sections)
        self.dump = tuple(dump)
        nc = bass.Bass("TRN2", target_bir_lowering=False)
        self.nc = nc
        self.inputs = set()
        self.dram = {}
        C = make_consts()
        self.E = Em(nc)
        self.d_in("x_in", [S, D], F32)
        self.dram["x_out"] = nc.dram_tensor("x_out", [S, D], F32, kind="ExternalOutput")
        self.xr = [nc.alloc_sbuf_tensor(f"xr{d}", [128, S], F32) for d in range(NDT)]
        self.xln = [nc.alloc_sbuf_tensor(f"xln{d}", [128, S], BF16) for d in range(NDT)]
        need = {"id128", "onesf", "negonesf"}
        if "attn0" in sections:
            need |= {"cosT", "ssinT", "perm", "caus", "iota8"}
        if "moe" in sections:
            need |= {"iota8", "crev8"}
        if "ffn1" in sections:
            need |= {"iota32", "crev32"}
        if "attn1" in sections:
            need |= {"mYq", "mZY", "mSU", "mSL", "mTri"}
        self.cs = {}
        self.cs_b16 = {}
        self.fc = nc.alloc_sbuf_tensor("fconst", [128, 4], F32)
        self.E.memset(self.fc[:, 0:1], 1e-5)
        self.E.memset(self.fc[:, 1:2], 1.0)
        self.E.memset(self.fc[:, 2:3], 1e-12)
        self.E.memset(self.fc[:, 3:4], 0.0)
        for n in sorted(need):
            arr = C[n]
            dt = self.d_in("c_" + n, list(arr.shape), F32)
            st = nc.alloc_sbuf_tensor("cs_" + n, list(arr.shape), F32)
            self.cs[n] = st
            self.E.dma("sync", st[:], dt[:])
            if n in ("id128", "onesf", "perm"):
                sb = nc.alloc_sbuf_tensor("cb_" + n, list(arr.shape), BF16)
                self.cs_b16[n] = sb
                self.E.dma("gpsimd", sb[:], dt[:])
        self.scopes = []
        self._uniq = {}

    def d_in(self, name, shape, dtype):
        h = self.nc.dram_tensor(name, shape, dtype, kind="ExternalInput")
        self.inputs.add(name)
        self.dram[name] = h
        return h

    def dbg_dump(self, t, name):
        od = self.nc.dram_tensor(name, list(t.shape), t.dtype, kind="ExternalOutput")
        self.E.dma("sync", od[:], t[:])

    def sb(self, name, shape, dtype):
        n = self._uniq.get(name, 0)
        self._uniq[name] = n + 1
        if n:
            name = f"{name}_{n}"
        cm = self.nc.sbuf_tensor(name, shape, dtype)
        t = cm.__enter__()
        self.scopes[-1].append(cm)
        return t

    def ps(self, name, shape, dtype=F32):
        n = self._uniq.get(name, 0)
        self._uniq[name] = n + 1
        if n:
            name = f"{name}_{n}"
        cm = self.nc.psum_tensor(name, shape, dtype)
        t = cm.__enter__()
        self.scopes[-1].append(cm)
        return t

    def begin_scope(self):
        self.scopes.append([])

    def end_scope(self):
        self.E.barrier()
        for cm in reversed(self.scopes.pop()):
            cm.__exit__(None, None, None)

    def load_x(self):
        E, nc = self.E, self.nc
        self.begin_scope()
        stg = [self.sb(f"ldstg{i}", [128, D], F32) for i in range(2)]
        pb = [self.ps(f"ldpb{i}", [128, 512], F32) for i in range(2)]
        xin = self.dram["x_in"]
        idf = self.cs["id128"]
        for tt in range(16):
            sg = stg[tt % 2]
            E.dma("sync", sg[:], xin[tt * 128:(tt + 1) * 128, :])
            for d in range(NDT):
                p = pb[d % 2]
                E.trans(p[:, 0:128], sg[:, d * 128:(d + 1) * 128], idf[:])
                E.copy(self.xr[d][:, tt * 128:(tt + 1) * 128], p[:, 0:128],
                       eng="scalar" if d % 2 else "vector")
        self.end_scope()

    def store_x(self):
        E, nc = self.E, self.nc
        self.begin_scope()
        stg = [self.sb(f"ststg{i}", [128, D], F32) for i in range(2)]
        pb = [self.ps(f"stpb{i}", [128, 512], F32) for i in range(2)]
        xout = self.dram["x_out"]
        idf = self.cs["id128"]
        for tt in range(16):
            sg = stg[tt % 2]
            for d in range(NDT):
                p = pb[d % 2]
                E.trans(p[:, 0:128], self.xr[d][:, tt * 128:(tt + 1) * 128], idf[:])
                E.copy(sg[:, d * 128:(d + 1) * 128], p[:, 0:128],
                       eng="scalar" if d % 2 else "vector")
            E.dma("sync", xout[tt * 128:(tt + 1) * 128, :], sg[:])
        self.end_scope()
        self.E.barrier()

    def sec_ln(self):
        E = self.E
        self.begin_scope()
        sq = [self.sb(f"lnsq{i}", [128, GW], F32) for i in range(2)]
        rows = self.sb("lnrows", [1, GW * 6], F32)
        pbs = self.ps("lnpbs", [128, 512], F32)
        pbq = self.ps("lnpbq", [128, 512], F32)
        pbr = self.ps("lnpbr", [128, 512], F32)
        pbm = self.ps("lnpbm", [128, 512], F32)
        onesf = self.cs["onesf"]
        t1 = [self.sb(f"lnt1{i}", [128, GW], F32) for i in range(2)]
        for g in range(NG):
            gs = slice(g * GW, (g + 1) * GW)
            for d in range(NDT):
                E.act(sq[d % 2][:], self.xr[d][:, gs], AF.Square, reads=[(self.xr[d], g)])
                E.mm(pbs[0:1, :], onesf[:, 0:1], self.xr[d][:, gs], d == 0, d == NDT - 1,
                     reads=[(self.xr[d], g)])
                E.mm(pbq[0:1, :], onesf[:, 0:1], sq[d % 2][:], d == 0, d == NDT - 1)
            mu = rows[0:1, 0:GW]
            ms = rows[0:1, GW:2 * GW]
            tt_ = rows[0:1, 2 * GW:3 * GW]
            var = rows[0:1, 3 * GW:4 * GW]
            r = rows[0:1, 4 * GW:5 * GW]
            mr = rows[0:1, 5 * GW:6 * GW]
            E.ts(mu, pbs[0:1, :], 1.0 / D, ALU.mult)
            E.ts(ms, pbq[0:1, :], 1.0 / D, ALU.mult)
            E.tt(tt_, mu, mu, ALU.mult)
            E.tt(var, ms, tt_, ALU.subtract)
            E.act(tt_, var, AF.Sqrt, bias=self.fc[0:1, 0:1])
            E.recip(r, tt_)
            E.tt(mr, mu, r, ALU.mult)
            E.mm(pbr[:, :], onesf[0:1, :], r, True, True)
            E.mm(pbm[:, :], onesf[0:1, :], mr, True, True)
            for d in range(NDT):
                E.tt(t1[d % 2][:], self.xr[d][:, gs], pbr[:, :], ALU.mult,
                     reads=[(self.xr[d], g)])
                E.tt(self.xln[d][:, gs], t1[d % 2][:], pbm[:, :], ALU.subtract,
                     writes=[(self.xln[d], g)])
        self.end_scope()


    def sec_moe(self):
        E, nc = self.E, self.nc
        w13 = self.dram.get("L0_ffn_moe_w13") or self.d_in("L0_ffn_moe_w13", [8, D, 4096], F32)
        w2 = self.dram.get("L0_ffn_moe_w2") or self.d_in("L0_ffn_moe_w2", [8, 2048, D], F32)
        rt = self.dram.get("L0_ffn_moe_router") or self.d_in("L0_ffn_moe_router", [D, 8], F32)
        self.begin_scope()
        onesf = self.cs["onesf"]
        iota8 = self.cs["iota8"]
        crev8 = self.cs["crev8"]
        id_f = self.cs["id128"]
        rsb = self.sb("rsb", [128, 64], BF16)
        E.dma("gpsimd", ap3(rsb, 0, 128, 0, [[8, 8], [1, 8]]),
              bass.AP(rt, 0, [[8, 128], [1024, 8], [1, 8]]))
        Wcall = self.sb("Wcall", [128, 128], F32)
        L = self.sb("Lrout", [128, 8], F32)
        t8a = self.sb("t8a", [128, 8], F32)
        t8b = self.sb("t8b", [128, 8], F32)
        oh1 = self.sb("oh1", [128, 8], F32)
        rwork = self.sb("rwork", [128, 8], F32)
        pb = [self.ps(f"moepb{i}", [128, 512], F32) for i in range(8)]
        m1 = rwork[:, 0:1]; m2 = rwork[:, 1:2]; ri = rwork[:, 2:3]; rib = rwork[:, 3:4]
        w1 = rwork[:, 4:5]; w2c = rwork[:, 5:6]; rd_ = rwork[:, 6:7]
        for c in range(16):
            for kt in range(NDT):
                E.mm(pb[0][:, 0:8], self.xln[kt][:, c * 128:(c + 1) * 128],
                     rsb[:, kt * 8:kt * 8 + 8], kt == 0, kt == NDT - 1,
                     reads=[(self.xln[kt], c // 4)])
            E.copy(L[:], pb[0][:, 0:8], eng="scalar")
            E.red(m1, L[:], ALU.max)
            E.ts(t8a[:], L[:], m1, ALU.is_equal)
            E.tt(t8b[:], t8a[:], crev8[:, 0:8], ALU.mult)
            E.red(ri, t8b[:], ALU.max)
            E.ts(rib, ri, -1.0, ALU.mult, 8.0, ALU.add)
            E.ts(oh1[:], iota8[:, 0:8], rib, ALU.is_equal)
            E.stt(t8a[:], oh1[:], -1e30, L[:], ALU.mult, ALU.add)
            E.red(m2, t8a[:], ALU.max)
            E.ts(t8b[:], t8a[:], m2, ALU.is_equal)
            E.tt(t8b[:], t8b[:], crev8[:, 0:8], ALU.mult)
            E.red(ri, t8b[:], ALU.max)
            E.ts(rib, ri, -1.0, ALU.mult, 8.0, ALU.add)
            E.ts(t8b[:], iota8[:, 0:8], rib, ALU.is_equal)
            E.tt(rd_, m1, m2, ALU.subtract)
            E.act(w1, rd_, AF.Sigmoid)
            E.ts(w2c, w1, -1.0, ALU.mult, 1.0, ALU.add)
            E.ts(t8a[:], oh1[:], w1, ALU.mult)
            E.stt(Wcall[:, c * 8:(c + 1) * 8], t8b[:], w2c, t8a[:], ALU.mult, ALU.add)
        HC = 256
        wgb = [self.sb(f"wgb{i}", [128, 2048], BF16) for i in range(2)]
        wub = [self.sb(f"wub{i}", [128, 2048], BF16) for i in range(2)]
        w2b = [self.sb(f"w2b{i}", [128, 512], BF16) for i in range(2)]
        wBs = [self.sb(f"wBs{i}", [128, GW], F32) for i in range(2)]
        Dg = self.sb("Dg", [128, 128], F32)
        sg = [self.sb(f"moesg{i}", [128, GW], BF16) for i in range(2)]
        hu = [self.sb(f"moehu{i}", [128, GW], BF16) for i in range(2)]
        hpg = [self.sb(f"hpg{i}", [128, 8192], BF16) for i in range(2)]
        macc = [self.sb(f"macc{i}", [128, 4096], F32) for i in range(2)]

        def dma_w13(e, hc, which, buf):
            coff = which * 2048 + hc * HC
            E.dma("gpsimd", ap3(buf, 0, 128, 0, [[HC, 8], [1, HC]]),
                  bass.AP(w13, e * D * 4096 + coff, [[4096, 128], [128 * 4096, 8], [1, HC]]))

        def dma_w2(e, dh, kt, buf):
            E.dma("gpsimd", buf[:],
                  bass.AP(w2, e * 2048 * D + kt * 128 * D + dh * 512, [[1024, 128], [1, 512]]))

        nhc = 2048 // HC
        for pair in range(2):
            gs = (2 * pair, 2 * pair + 1)
            for e in range(8):
                for gi, g in enumerate(gs):
                    for cc in range(4):
                        c = g * 4 + cc
                        E.ts(Dg[:], id_f[:], Wcall[:, c * 8 + e:c * 8 + e + 1], ALU.mult)
                        E.mm(pb[7][:, cc * 128:(cc + 1) * 128], onesf[:], Dg[:], True, True)
                    E.copy(wBs[gi][:], pb[7][:], eng="vector")
                dma_w13(e, 0, 0, wgb[0])
                dma_w13(e, 0, 1, wub[0])
                for hc in range(nhc):
                    wg, wu = wgb[hc % 2], wub[hc % 2]
                    if hc + 1 < nhc:
                        dma_w13(e, hc + 1, 0, wgb[(hc + 1) % 2])
                        dma_w13(e, hc + 1, 1, wub[(hc + 1) % 2])
                    for gi, g in enumerate(gs):
                        gsl = slice(g * GW, (g + 1) * GW)
                        for mt in range(2):
                            for kt in range(NDT):
                                E.mm(pb[mt][:], wg[:, kt * HC + mt * 128:kt * HC + (mt + 1) * 128],
                                     self.xln[kt][:, gsl], kt == 0, kt == NDT - 1,
                                     reads=[(self.xln[kt], g)])
                                E.mm(pb[2 + mt][:], wu[:, kt * HC + mt * 128:kt * HC + (mt + 1) * 128],
                                     self.xln[kt][:, gsl], kt == 0, kt == NDT - 1,
                                     reads=[(self.xln[kt], g)])
                        for mt in range(2):
                            E.act(sg[mt][:], pb[mt][:], AF.Silu)
                            E.tt(hu[mt][:], sg[mt][:], pb[2 + mt][:], ALU.mult)
                            E.tt(hpg[gi][:, (hc * 2 + mt) * GW:(hc * 2 + mt + 1) * GW],
                                 hu[mt][:], wBs[gi][:], ALU.mult)
                for dh in range(2):
                    dma_w2(e, dh, 0, w2b[0])
                    for kt in range(16):
                        if kt + 1 < 16:
                            dma_w2(e, dh, kt + 1, w2b[(kt + 1) % 2])
                        for gi in range(2):
                            for mt in range(4):
                                E.mm(pb[gi * 4 + mt][:],
                                     w2b[kt % 2][:, mt * 128:(mt + 1) * 128],
                                     hpg[gi][:, kt * GW:(kt + 1) * GW],
                                     kt == 0, kt == 15)
                    for gi in range(2):
                        for mt in range(4):
                            dst = macc[gi][:, (dh * 4 + mt) * GW:(dh * 4 + mt + 1) * GW]
                            if e == 0:
                                E.copy(dst, pb[gi * 4 + mt][:], eng="vector")
                            else:
                                E.tt(dst, dst, pb[gi * 4 + mt][:], ALU.add)
            for gi, g in enumerate(gs):
                gsl = slice(g * GW, (g + 1) * GW)
                for d in range(NDT):
                    E.stt(self.xr[d][:, gsl], macc[gi][:, d * GW:(d + 1) * GW], 0.5,
                          self.xr[d][:, gsl], ALU.mult, ALU.add)
        self.end_scope()


    def sec_attn0(self):
        E, nc = self.E, self.nc
        Wq = self.dram.get("L0_attn_Wq") or self.d_in("L0_attn_Wq", [D, D], F32)
        bq = self.dram.get("L0_attn_bq") or self.d_in("L0_attn_bq", [D], F32)
        Wkv = self.dram.get("L0_attn_Wkv") or self.d_in("L0_attn_Wkv", [D, 2048], F32)
        bkv = self.dram.get("L0_attn_bkv") or self.d_in("L0_attn_bkv", [2048], F32)
        Wo = self.dram.get("L0_attn_Wo") or self.d_in("L0_attn_Wo", [D, D], F32)
        bo = self.dram.get("L0_attn_bo") or self.d_in("L0_attn_bo", [D], F32)
        self.begin_scope()
        onesf = self.cs["onesf"]
        onesb = self.cs_b16["onesf"]
        idb = self.cs_b16["id128"]
        permb = self.cs_b16["perm"]
        cosT = self.cs["cosT"]
        ssinT = self.cs["ssinT"]
        caus = self.cs["caus"]
        VW = 520
        qh = self.sb("a0q", [128, 4 * S], BF16)
        kh = self.sb("a0k", [128, 4 * S], BF16)
        vTh = self.sb("a0vT", [128, 16 * VW], BF16)
        vstage = self.sb("a0vs", [128, GW], BF16)
        tmpA = self.sb("a0tmpA", [128, S], BF16)
        tmpB = [self.sb("a0tmpB0", [128, GW], BF16)] * 2
        expb = [self.sb(f"a0exp{i}", [128, GW], BF16) for i in range(2)]
        mskt = self.sb("a0mskt", [128, GW], F32)
        negc = self.sb("a0negc", [128, GW], BF16)
        qn = self.sb("a0qn", [1, GW], F32)
        nrow = self.sb("a0nrow", [1, GW], F32)
        kmx = self.sb("a0kmx", [1, 64], F32)
        rdenb = self.sb("a0rden", [64, GW], F32)
        oblk = self.sb("a0oblk", [128, 4 * GW], BF16)
        brow = self.sb("a0brow", [1, 4096], BF16)
        onesw = self.sb("a0onesw", [1, GW], BF16)
        wbuf = [self.sb(f"a0wb{i}", [128, GW], BF16) for i in range(2)]
        wob = wbuf
        pb = [self.ps(f"a0pb{i}", [128, 512], F32) for i in range(5)]
        pbT = self.ps("a0pbT", [128, 1024], BF16)
        E.memset(onesw[:], 1.0)
        E.dma("gpsimd", brow[0:1, 0:1024], bass.AP(bq, 0, [[1024, 1], [1, 1024]]))
        E.dma("gpsimd", brow[0:1, 1024:3072], bass.AP(bkv, 0, [[2048, 1], [1, 2048]]))
        E.dma("gpsimd", brow[0:1, 3072:4096], bass.AP(bo, 0, [[1024, 1], [1, 1024]]))
        E.memset(ap3(vTh, 0, 128, 64, [[VW, 16], [65, 8]]), 1.0)

        def proj_pass(half, wdram, wcol, dst, bias_off, scale):
            for g in range(NG):
                gsl = slice(g * GW, (g + 1) * GW)
                for kt in range(NDT):
                    wb = wbuf[kt % 2]
                    E.dma("gpsimd", wb[:],
                          bass.AP(wdram, kt * 128 * wdram.shape[1] + wcol,
                                  [[wdram.shape[1], 128], [1, 512]]))
                    for mt in range(4):
                        E.mm(pb[mt][:], wb[:, mt * 128:(mt + 1) * 128],
                             self.xln[kt][:, gsl], kt == 0, False,
                             reads=[(self.xln[kt], g)])
                for mt in range(4):
                    E.mm(pb[mt][:], brow[0:1, bias_off + mt * 128:bias_off + (mt + 1) * 128],
                         onesw[0:1, :], False, True)
                    E.act(dst[:, mt * S + g * GW:mt * S + (g + 1) * GW], pb[mt][:],
                          AF.Identity, scale=scale)

        def rope(dst):
            for mt in range(4):
                tsl = slice(mt * S, (mt + 1) * S)
                E.tt(tmpA[:], dst[:, tsl], cosT[:], ALU.mult)
                for gg in range(4):
                    E.mm(pb[gg][:], permb[:], dst[:, mt * S + gg * GW:mt * S + (gg + 1) * GW],
                         True, True)
                for gg in range(4):
                    E.tt(tmpB[gg % 2][:], pb[gg][:], ssinT[:, gg * GW:(gg + 1) * GW], ALU.mult)
                    E.tt(dst[:, mt * S + gg * GW:mt * S + (gg + 1) * GW],
                         tmpA[:, gg * GW:(gg + 1) * GW], tmpB[gg % 2][:], ALU.add)

        for half in range(2):
            proj_pass(half, Wq, half * 512, qh, half * 512, 0.125)
            proj_pass(half, Wkv, half * 512, kh, 1024 + half * 512, 1.0)
            rope(qh)
            rope(kh)
            for g in range(NG):
                gsl = slice(g * GW, (g + 1) * GW)
                for kt in range(NDT):
                    wb = wbuf[kt % 2]
                    E.dma("gpsimd", wb[:],
                          bass.AP(Wkv, kt * 128 * 2048 + 1024 + half * 512,
                                  [[2048, 128], [1, 512]]))
                    for mt in range(4):
                        E.mm(pb[mt][:], wb[:, mt * 128:(mt + 1) * 128],
                             self.xln[kt][:, gsl], kt == 0, False,
                             reads=[(self.xln[kt], g)])
                for mt in range(4):
                    E.mm(pb[mt][:], brow[0:1, 2048 + half * 512 + mt * 128:2048 + half * 512 + (mt + 1) * 128],
                         onesw[0:1, :], False, True)
                    E.act(vstage[:], pb[mt][:], AF.Identity)
                    for b in range(4):
                        E.trans(pbT[:, b * 128:(b + 1) * 128], vstage[:, b * 128:(b + 1) * 128],
                                idb[:])
                    for b in range(4):
                        ti = 4 * g + b
                        E.copy(vTh[:, ti * VW + (2 * mt) * 65:ti * VW + (2 * mt) * 65 + 64],
                               pbT[:, b * 128:b * 128 + 64], eng="vector")
                        E.copy(vTh[:, ti * VW + (2 * mt + 1) * 65:ti * VW + (2 * mt + 1) * 65 + 64],
                               pbT[:, b * 128 + 64:(b + 1) * 128], eng="scalar")
            for hl in range(8):
                mt, ro = hl // 2, 64 * (hl % 2)
                for gg in range(4):
                    E.act(tmpA[ro:ro + 64, 0:GW],
                          kh[ro:ro + 64, mt * S + gg * GW:mt * S + (gg + 1) * GW],
                          AF.Square)
                    E.mm(pb[4][0:1, :], onesb[ro:ro + 64, 0:1],
                         tmpA[ro:ro + 64, 0:GW], True, True)
                    E.red(kmx[0:1, hl * 4 + gg:hl * 4 + gg + 1], pb[4][0:1, :], ALU.max)
                E.red(kmx[0:1, 32 + hl:32 + hl + 1], kmx[0:1, hl * 4:hl * 4 + 4], ALU.max)
                E.act(kmx[0:1, 48 + hl:48 + hl + 1], kmx[0:1, 32 + hl:32 + hl + 1], AF.Sqrt)
            for g in range(NG):
                for hl in range(8):
                    mt, ro = hl // 2, 64 * (hl % 2)
                    E.act(tmpA[ro:ro + 64, 0:GW],
                          qh[ro:ro + 64, mt * S + g * GW:mt * S + (g + 1) * GW], AF.Square)
                    E.mm(pb[4][0:1, :], onesb[ro:ro + 64, 0:1],
                         tmpA[ro:ro + 64, 0:GW], True, True)
                    E.act(qn[:], pb[4][0:1, :], AF.Sqrt)
                    E.ts(negc[ro:ro + 1, :], qn[:], kmx[0:1, 48 + hl:48 + hl + 1], ALU.mult,
                         -1.0, ALU.mult)
                    ni = 4 * g + 4
                    for i in range(ni):
                        sb_ = pb[i % 2]
                        E.mm(sb_[:], kh[ro:ro + 64, mt * S + i * 128:mt * S + (i + 1) * 128],
                             qh[ro:ro + 64, mt * S + g * GW:mt * S + (g + 1) * GW], True, False)
                        E.mm(sb_[:], onesb[ro:ro + 1, 0:128], negc[ro:ro + 1, :], False, True)
                        if i >= 4 * g:
                            j = i - 4 * g
                            E.tt(mskt[:], sb_[:], caus[:, j * GW:(j + 1) * GW], ALU.add)
                            E.act(expb[i % 2][:], mskt[:], AF.Exp)
                        else:
                            E.act(expb[i % 2][:], sb_[:], AF.Exp)
                        E.mm(pb[2][0:65, :], vTh[:, i * VW + hl * 65:i * VW + (hl + 1) * 65],
                             expb[i % 2][:], i == 0, i == ni - 1)
                    E.recip(nrow[:], pb[2][64:65, :])
                    E.mm(pb[3][0:64, :], onesf[0:1, 0:64], nrow[:], True, True)
                    E.copy(rdenb[:], pb[3][0:64, :], eng="scalar")
                    E.tt(oblk[ro:ro + 64, mt * GW:(mt + 1) * GW],
                         pb[2][0:64, :], rdenb[:], ALU.mult)
                for dh in range(2):
                    for kt in range(4):
                        wb = wob[kt % 2]
                        E.dma("gpsimd", wb[:],
                              bass.AP(Wo, (half * 512 + kt * 128) * D + dh * 512,
                                      [[D, 128], [1, 512]]))
                        for mt in range(4):
                            E.mm(pb[mt][:], wb[:, mt * 128:(mt + 1) * 128],
                                 oblk[:, kt * GW:(kt + 1) * GW], kt == 0,
                                 kt == 3 and half == 1)
                    for mt in range(4):
                        if half == 0:
                            E.mm(pb[mt][:], brow[0:1, 3072 + dh * 512 + mt * 128:3072 + dh * 512 + (mt + 1) * 128],
                                 onesw[0:1, :], False, True)
                        d = dh * 4 + mt
                        gsl = slice(g * GW, (g + 1) * GW)
                        E.tt(self.xr[d][:, gsl], self.xr[d][:, gsl], pb[mt][:], ALU.add)
        self.end_scope()


    def sec_ffn1(self):
        E, nc = self.E, self.nc
        rw = self.dram.get("L1_ffn_router_w") or self.d_in("L1_ffn_router_w", [32, D], F32)
        rb = self.dram.get("L1_ffn_router_b") or self.d_in("L1_ffn_router_b", [32], F32)
        w1 = self.dram.get("L1_ffn_weights1") or self.d_in("L1_ffn_weights1", [32, D, 256], F32)
        b1 = self.dram.get("L1_ffn_bias1") or self.d_in("L1_ffn_bias1", [32, 256], F32)
        w2 = self.dram.get("L1_ffn_weights2") or self.d_in("L1_ffn_weights2", [32, 256, D], F32)
        b2 = self.dram.get("L1_ffn_bias2") or self.d_in("L1_ffn_bias2", [32, D], F32)
        self.begin_scope()
        onesb = self.cs_b16["onesf"]
        idb = self.cs_b16["id128"]
        idf = self.cs["id128"]
        iota32 = self.cs["iota32"]
        crev32 = self.cs["crev32"]
        ohcall = self.sb("f1ohc", [128, 512], F32)
        OHem = self.sb("f1ohem", [32, S], BF16)
        ohBs = self.sb("f1ohbs", [128, GW], BF16)
        b2sb = self.sb("f1b2", [32, D], BF16)
        routerT = self.sb("f1rT", [128, 256], BF16)
        onesw = self.sb("f1ones", [1, GW], BF16)
        L32 = self.sb("f1L", [128, 32], F32)
        t32a = self.sb("f1ta", [128, 32], F32)
        rwk = self.sb("f1rw", [128, 4], F32)
        ohb = self.sb("f1ohb", [128, 32], BF16)
        rbrow = self.sb("f1rb", [1, 32], BF16)
        hb = [self.sb(f"f1hb{i}", [128, 2 * GW], BF16) for i in range(4)]
        b1row = [self.sb(f"f1b1r{i}", [1, 256], BF16) for i in range(2)]
        NEB = 4
        w1blk = self.sb("f1w1", [128, NEB * 2048], BF16)
        w2blk = self.sb("f1w2", [128, NEB * 2048], BF16)
        zacc = self.sb("f1z", [128, NDT * S], BF16)
        Dg = self.sb("f1dg", [128, 128], F32)
        pb = [self.ps(f"f1pb{i}", [128, 512], F32) for i in range(6)]
        pbT = self.ps("f1pbT", [128, 1024], BF16)
        E.memset(onesw[:], 1.0)
        E.dma("gpsimd", b2sb[:], b2[:])
        E.dma("gpsimd", rbrow[0:1, :], bass.AP(rb, 0, [[32, 1], [1, 32]]))
        self.begin_scope()
        rsb = self.sb("f1rsb", [32, D], F32)
        E.dma("sync", rsb[:], rw[:])
        for kt in range(NDT):
            E.trans(pb[5][0:128, 0:32], rsb[0:32, kt * 128:(kt + 1) * 128], idf[0:32, 0:32])
            E.copy(routerT[:, kt * 32:(kt + 1) * 32], pb[5][0:128, 0:32], eng="scalar")
        self.end_scope()
        m1 = rwk[:, 0:1]; ri = rwk[:, 1:2]; rib = rwk[:, 2:3]
        for c in range(16):
            for kt in range(NDT):
                E.mm(pb[5][:, 0:32], self.xln[kt][:, c * 128:(c + 1) * 128],
                     routerT[:, kt * 32:(kt + 1) * 32], kt == 0, False,
                     reads=[(self.xln[kt], c // 4)])
            E.mm(pb[5][:, 0:32], onesb[0:1, 0:128], rbrow[0:1, :], False, True)
            E.copy(L32[:], pb[5][:, 0:32], eng="scalar")
            E.red(m1, L32[:], ALU.max)
            E.ts(t32a[:], L32[:], m1, ALU.is_equal)
            E.tt(t32a[:], t32a[:], crev32[:], ALU.mult)
            E.red(ri, t32a[:], ALU.max)
            E.ts(rib, ri, -1.0, ALU.mult, 32.0, ALU.add)
            E.ts(ohcall[:, c * 32:(c + 1) * 32], iota32[:], rib, ALU.is_equal)
            E.copy(ohb[:], ohcall[:, c * 32:(c + 1) * 32])
            E.trans(pbT[0:32, 0:128], ohb[:], idb[:])
            E.copy(OHem[:, c * 128:(c + 1) * 128], pbT[0:32, 0:128], eng="scalar")
        for eb in range(32 // NEB):
            for i in range(NEB):
                e = eb * NEB + i
                E.dma("gpsimd", ap3(w1blk, 0, 128, i * 2048, [[256, 8], [1, 256]]),
                      bass.AP(w1, e * D * 256, [[256, 128], [128 * 256, 8], [1, 256]]))
                E.dma("gpsimd", ap3(w2blk, 0, 128, i * 2048, [[1024, 2], [1, 1024]]),
                      bass.AP(w2, e * 256 * D, [[1024, 128], [128 * 1024, 2], [1, 1024]]))
            for g in range(NG):
                gsl = slice(g * GW, (g + 1) * GW)
                for i in range(NEB):
                    e = eb * NEB + i
                    E.dma("gpsimd", b1row[i % 2][0:1, :],
                          bass.AP(b1, e * 256, [[256, 1], [1, 256]]))
                    for mt in range(2):
                        hbank = pb[0] if mt == 0 else pb[5]
                        for kt in range(NDT):
                            E.mm(hbank[:],
                                 w1blk[:, i * 2048 + kt * 256 + mt * 128:i * 2048 + kt * 256 + (mt + 1) * 128],
                                 self.xln[kt][:, gsl], kt == 0, False,
                                 reads=[(self.xln[kt], g)])
                        E.mm(hbank[:],
                             b1row[i % 2][0:1, mt * 128:(mt + 1) * 128],
                             onesw[0:1, :], False, True)
                        E.act(hb[i][:, mt * GW:(mt + 1) * GW], hbank[:], AF.Relu)
                    for cc in range(4):
                        c = g * 4 + cc
                        E.ts(Dg[:], idf[:], ohcall[:, c * 32 + e:c * 32 + e + 1], ALU.mult)
                        E.mm(pb[5][:, cc * 128:(cc + 1) * 128], self.cs["onesf"][:],
                             Dg[:], True, True)
                    E.copy(ohBs[:], pb[5][:])
                    for mt in range(2):
                        E.tt(hb[i][:, mt * GW:(mt + 1) * GW],
                             hb[i][:, mt * GW:(mt + 1) * GW], ohBs[:], ALU.mult)
                for dh in range(2):
                    for dt in range(4):
                        dd = dh * 4 + dt
                        if eb == 0:
                            E.mm(pb[1 + dt][:], b2sb[0:32, dd * 128:(dd + 1) * 128],
                                 OHem[0:32, gsl], True, False)
                        for i in range(NEB):
                            for kt in range(2):
                                E.mm(pb[1 + dt][:],
                                     w2blk[:, i * 2048 + kt * 1024 + dd * 128:i * 2048 + kt * 1024 + (dd + 1) * 128],
                                     hb[i][:, kt * GW:(kt + 1) * GW],
                                     eb != 0 and i == 0 and kt == 0,
                                     i == NEB - 1 and kt == 1)
                        dst = zacc[:, dd * S + g * GW:dd * S + (g + 1) * GW]
                        if eb == 0:
                            E.copy(dst, pb[1 + dt][:])
                        else:
                            E.tt(dst, dst, pb[1 + dt][:], ALU.add)
        for d in range(NDT):
            for g in range(NG):
                gsl = slice(g * GW, (g + 1) * GW)
                E.act(ohBs[:], zacc[:, d * S + g * GW:d * S + (g + 1) * GW], AF.Relu)
                E.stt(self.xr[d][:, gsl], ohBs[:], 0.5, self.xr[d][:, gsl],
                      ALU.mult, ALU.add)
        self.end_scope()


    def sec_attn1(self):
        E, nc = self.E, self.nc
        dn = {}
        for nm, shp in (("Wq", [D, D]), ("Wk", [D, D]), ("Wv", [D, D]),
                        ("Wf1", [D, 64]), ("Wf2", [64, D]), ("Wg", [D, 16]),
                        ("Wo1", [D, 64]), ("Wo2", [64, D]), ("ln_w", [D]),
                        ("Wout", [D, D])):
            key = "L1_attn_" + nm
            dn[nm] = self.dram.get(key) or self.d_in(key, shp, F32)
        self.begin_scope()
        onesf = self.cs["onesf"]
        onesb = self.cs_b16["onesf"]
        idb = self.cs_b16["id128"]
        idf = self.cs["id128"]
        mYq = self.cs["mYq"]
        mZY = self.cs["mZY"]
        mSU = self.cs["mSU"]
        mSL = self.cs["mSL"]
        mTri = self.cs["mTri"]
        CH0 = A1_CH0
        NCH = A1_NCH
        OW = max(1, NCH - CH0) * 64
        Sf = [self.sb(f"a1Sf{i}", [128, 64], F32) for i in range(4)]
        Sb = [self.sb(f"a1Sb{i}", [128, 64], BF16) for i in range(4)]
        OTall = self.sb("a1OT", [128, NDT * OW], BF16)
        y1b = self.sb("a1y1", [64, S], BF16)
        y2b = self.sb("a1y2", [64, S], BF16)
        gtok = self.sb("a1gtok", [128, 256], F32)
        wsm = self.sb("a1wsm", [128, 512 + 128], BF16)
        pA = self.ps("a1pA", [128, 512], F32)
        pM = self.ps("a1pM", [128, 512], F32)
        pX = self.ps("a1pX", [128, 512], F32)
        pC = self.ps("a1pC", [128, 512], F32)
        pD = self.ps("a1pD", [128, 512], F32)
        pE = self.ps("a1pE", [128, 512], F32)
        pF = self.ps("a1pF", [128, 512], F32)
        pT = self.ps("a1pT", [128, 1024], BF16)
        E.dma("gpsimd", ap3(wsm, 0, 128, 0, [[64, 8], [1, 64]]),
              bass.AP(dn["Wf1"], 0, [[64, 128], [128 * 64, 8], [1, 64]]))
        E.dma("gpsimd", ap3(wsm, 0, 128, 512, [[16, 8], [1, 16]]),
              bass.AP(dn["Wg"], 0, [[16, 128], [128 * 16, 8], [1, 16]]))
        self.begin_scope()
        wo1b = self.sb("a1wo1", [128, 512], BF16)
        gml = self.sb("a1gml", [16, S], F32)
        E.dma("gpsimd", ap3(wo1b, 0, 128, 0, [[64, 8], [1, 64]]),
              bass.AP(dn["Wo1"], 0, [[64, 128], [128 * 64, 8], [1, 64]]))
        for g in range(NG):
            if A1_SUB < 1:
                break
            gsl = slice(g * GW, (g + 1) * GW)
            for kt in range(NDT):
                E.mm(pA[0:64, :], wsm[:, kt * 64:(kt + 1) * 64],
                     self.xln[kt][:, gsl], kt == 0, kt == NDT - 1,
                     reads=[(self.xln[kt], g)])
                E.mm(pM[0:64, :], wo1b[:, kt * 64:(kt + 1) * 64],
                     self.xln[kt][:, gsl], kt == 0, kt == NDT - 1,
                     reads=[(self.xln[kt], g)])
                E.mm(pX[0:16, :], wsm[:, 512 + kt * 16:512 + (kt + 1) * 16],
                     self.xln[kt][:, gsl], kt == 0, kt == NDT - 1,
                     reads=[(self.xln[kt], g)])
            if A1_SUB < 2:
                continue
            E.act(y1b[0:64, gsl], pA[0:64, :], AF.Identity)
            E.act(y2b[0:64, gsl], pM[0:64, :], AF.Identity)
            E.act(gml[0:16, gsl], pX[0:16, :], AF.Sigmoid)
        for tp in range(16):
            if A1_SUB < 3:
                break
            E.trans(pF[0:128, 0:16], gml[0:16, tp * 128:(tp + 1) * 128], idf[0:16, 0:16])
            E.copy(gtok[:, tp * 16:(tp + 1) * 16], pF[0:128, 0:16], eng="scalar")
        self.end_scope()

        for half in range(A1_HALVES):
            self.begin_scope()
            qs = self.sb("a1qs", [128, 4 * S], BF16)
            ks = self.sb("a1ks", [128, 4 * S], BF16)
            vTs = self.sb("a1vTs", [128, 16 * 512], BF16)
            vstage = self.sb("a1vst", [128, GW], BF16)
            wbuf = [self.sb(f"a1wb{i}", [128, GW], BF16) for i in range(2)]
            wf2b = self.sb("a1wf2", [64, 512], BF16)
            nrow = self.sb("a1nrow", [1, GW], F32)
            sqb = self.sb("a1sqb", [128, GW], BF16)
            E.dma("gpsimd", wf2b[0:64, :],
                  bass.AP(dn["Wf2"], half * 512, [[D, 64], [1, 512]]))

            def proj(wdram, dst):
                for g in range(NG):
                    gsl = slice(g * GW, (g + 1) * GW)
                    for kt in range(NDT):
                        wb = wbuf[kt % 2]
                        E.dma("gpsimd", wb[:],
                              bass.AP(wdram, kt * 128 * D + half * 512, [[D, 128], [1, 512]]))
                        for mt in range(4):
                            E.mm(pb4[mt][:], wb[:, mt * 128:(mt + 1) * 128],
                                 self.xln[kt][:, gsl], kt == 0, kt == NDT - 1,
                                 reads=[(self.xln[kt], g)])
                    for mt in range(4):
                        if dst is None:
                            E.act(vstage[:], pb4[mt][:], AF.Silu)
                            for b in range(4):
                                E.trans(pT[:, b * 128:(b + 1) * 128],
                                        vstage[:, b * 128:(b + 1) * 128], idb[:])
                            for b in range(4):
                                ti = 4 * g + b
                                E.copy(vTs[:, ti * 512 + (2 * mt) * 64:ti * 512 + (2 * mt) * 64 + 64],
                                       pT[:, b * 128:b * 128 + 64], eng="vector")
                                E.copy(vTs[:, ti * 512 + (2 * mt + 1) * 64:ti * 512 + (2 * mt + 1) * 64 + 64],
                                       pT[:, b * 128 + 64:(b + 1) * 128], eng="scalar")
                        else:
                            E.act(dst[:, mt * S + g * GW:mt * S + (g + 1) * GW],
                                  pb4[mt][:], AF.Silu)

            pb4 = [pA, pM, pX, pE]
            proj(dn["Wq"], qs)
            proj(dn["Wk"], ks)
            proj(dn["Wv"], None)
            for hl in range(8):
                mt, ro = hl // 2, 64 * (hl % 2)
                for gg in range(4):
                    E.act(sqb[ro:ro + 64, :],
                          ks[ro:ro + 64, mt * S + gg * GW:mt * S + (gg + 1) * GW], AF.Square)
                    E.mm(pC[0:1, :], onesb[ro:ro + 64, 0:1], sqb[ro:ro + 64, :], True, True)
                    E.act(nrow[:], pC[0:1, :], AF.Sqrt)
                    E.ts(nrow[:], nrow[:], 1e-12, ALU.max)
                    E.recip(nrow[:], nrow[:])
                    E.mm(pD[ro:ro + 64, :], onesf[0:1, 0:64], nrow[:], True, True)
                    E.tt(ks[ro:ro + 64, mt * S + gg * GW:mt * S + (gg + 1) * GW],
                         ks[ro:ro + 64, mt * S + gg * GW:mt * S + (gg + 1) * GW],
                         pD[ro:ro + 64, :], ALU.mult)
            if A1_DBG and half == 0:
                self.dbg_dump(qs, "dbg_qs")
                self.dbg_dump(ks, "dbg_ks")
                self.dbg_dump(vTs, "dbg_vTs")
                self.dbg_dump(y1b, "dbg_y1b")
                self.dbg_dump(gtok, "dbg_gtok")
            for i in range(4):
                E.memset(Sf[i][:], 0.0)
                E.memset(Sb[i][:], 0.0)
            fch = self.sb("a1fch", [128, 256], F32)
            Lch = self.sb("a1Lch", [128, 256], F32)
            Ech = self.sb("a1Ech", [128, 256], F32)
            Einv = self.sb("a1Einv", [128, 256], F32)
            Epch = self.sb("a1Epch", [128, 256], F32)
            Epinv = self.sb("a1Epinv", [128, 256], F32)
            ft1 = self.sb("a1ft1", [128, 256], F32)
            ft2 = self.sb("a1ft2", [128, 256], F32)
            onec = self.sb("a1onec", [128, 64], F32)
            kgb = self.sb("a1kgb", [128, 256], BF16)
            Bch = self.sb("a1Bch", [128, 512], BF16)
            Cmch = self.sb("a1Cmch", [128, 512], BF16)
            Qch = self.sb("a1Qch", [128, 256], BF16)
            Cbch = self.sb("a1Cbch", [128, 256], BF16)
            Dg2 = self.sb("a1Dg2", [128, 64], F32)
            M1Tb = self.sb("a1M1T", [128, 128], BF16)
            M1b = self.sb("a1M1", [128, 128], BF16)
            PTp = [self.sb(f"a1PT{i}", [128, 128], BF16) for i in range(2)]
            Pp = [self.sb(f"a1P{i}", [128, 128], BF16) for i in range(2)]
            Xb = self.sb("a1Xb", [128, 192], BF16)
            Wb = self.sb("a1Wb", [128, 128], BF16)
            CmTb = self.sb("a1CmT", [128, 64], BF16)
            CbTb = self.sb("a1CbT", [128, 64], BF16)
            CbPTb = self.sb("a1CbPT", [128, 64], BF16)
            aTb = self.sb("a1aTb", [128, 64], BF16)
            ot1 = self.sb("a1ot1", [128, 64], F32)
            st1 = self.sb("a1st1", [128, 64], F32)
            E.memset(onec[:], 1.0)
            qs3 = lambda c0: ap3(qs, 0, 128, c0, [[S, 4], [1, 64]])
            ks3 = lambda c0: ap3(ks, 0, 128, c0, [[S, 4], [1, 64]])
            for ch in range(NCH):
                ro2 = 64 * (ch % 2)
                c0 = ch * 64
                for mtl in range(4):
                    E.mm(pA[:, mtl * 64:(mtl + 1) * 64],
                         wf2b[0:64, mtl * 128:(mtl + 1) * 128],
                         y1b[0:64, c0:c0 + 64], mtl == mtl, True)
                E.act(ft1[:], pA[:, 0:256], AF.Abs)
                E.act(ft1[:], ft1[:], AF.Exp, scale=-1.0)
                E.act(ft1[:], ft1[:], AF.Ln, bias=self.fc[:, 1:2])
                E.ts(ft2[:], pA[:, 0:256], 0.0, ALU.min)
                E.tt(fch[:], ft2[:], ft1[:], ALU.subtract)
                for mtl in range(4):
                    E.scan(Lch[:, mtl * 64:(mtl + 1) * 64], onec[:],
                           fch[:, mtl * 64:(mtl + 1) * 64], 0.0, ALU.mult, ALU.add)
                E.act(Ech[:], Lch[:], AF.Exp)
                E.act(Einv[:], Lch[:], AF.Exp, scale=-1.0)
                E.tt(ft1[:], Lch[:], fch[:], ALU.subtract)
                E.act(Epch[:], ft1[:], AF.Exp)
                E.act(Epinv[:], ft1[:], AF.Exp, scale=-1.0)
                for hl in range(8 if A1_SUB2 >= 2 else 0):
                    mtl, ro = hl // 2, 64 * (hl % 2)
                    gcol = gtok[ro2:ro2 + 64, (ch // 2) * 16 + half * 8 + hl:(ch // 2) * 16 + half * 8 + hl + 1]
                    E.ts(Dg2[ro2:ro2 + 64, :], idf[ro2:ro2 + 64, ro2:ro2 + 64], gcol,
                         ALU.mult, -1.0, ALU.mult)
                    E.mm(pA[ro:ro + 64, mtl * 64:(mtl + 1) * 64],
                         onesf[ro2:ro2 + 64, 0:64], Dg2[ro2:ro2 + 64, :], True, True)
                if A1_SUB2 < 3:
                    continue
                E.tt(Qch[:], qs3(c0), Ech[:, 0:256], ALU.mult)
                E.tt(Cbch[:], ks3(c0), Einv[:, 0:256], ALU.mult)
                E.tt(kgb[:], ks3(c0), pA[:, 0:256], ALU.mult)
                E.tt(ap3(Bch, 0, 128, 0, [[128, 4], [2, 64]]), ks3(c0), Epch[:, 0:256], ALU.mult)
                E.tt(ap3(Bch, 0, 128, 1, [[128, 4], [2, 64]]), ks3(c0), Ech[:, 0:256], ALU.mult)
                E.tt(ap3(Cmch, 0, 128, 0, [[128, 4], [2, 64]]), kgb[:], Epinv[:, 0:256], ALU.mult)
                E.tt(ap3(Cmch, 0, 128, 1, [[128, 4], [2, 64]]), kgb[:], Einv[:, 0:256], ALU.mult)
                for hl in range(8 if A1_SUB2 >= 4 else 0):
                    mtl, ro = hl // 2, 64 * (hl % 2)
                    Bh = Bch[ro:ro + 64, mtl * 128:(mtl + 1) * 128]
                    Cmh = Cmch[ro:ro + 64, mtl * 128:(mtl + 1) * 128]
                    Cbh = Cbch[ro:ro + 64, mtl * 64:(mtl + 1) * 64]
                    Qh = Qch[ro:ro + 64, mtl * 64:(mtl + 1) * 64]
                    Vh = vTs[ro2:ro2 + 64, (ch // 2) * 512 + hl * 64:(ch // 2) * 512 + (hl + 1) * 64]
                    Sfh = Sf[mtl][ro:ro + 64, :]
                    Sbh = Sb[mtl][ro:ro + 64, :]
                    E.mm(pM[:, 0:128], Cmh, Bh, True, True)
                    E.tt(M1Tb[:], pM[:, 0:128], mSU[:], ALU.mult)
                    E.mm(pM[:, 128:256], Bh, Cmh, True, True)
                    E.tt(M1b[:], pM[:, 128:256], mSL[:], ALU.mult)
                    E.mm(pX[:, 0:64], Cmh, Qh, True, True)
                    E.mm(pX[:, 64:128], Bh, Cbh, True, True)
                    E.mm(pX[:, 128:192], Bh, Sbh, True, True)
                    E.tt(Xb[:], pX[:, 0:192], mZY[:], ALU.mult)
                    if A1_SUB2 < 5:
                        continue
                    E.copy(Wb[:], Xb[:, 64:192])
                    for k in range(7):
                        PTk = M1Tb if k == 0 else PTp[k % 2]
                        Pk = M1b if k == 0 else Pp[k % 2]
                        E.mm(pC[:, 0:128], PTk[:], Wb[:], True, True)
                        E.tt(Wb[:], pC[:, 0:128], Wb[:], ALU.add)
                        if k < 6:
                            E.mm(pC[:, 128:256], PTk[:], Pk[:], True, True)
                            E.copy(Pp[(k + 1) % 2][:], pC[:, 128:256], eng="scalar")
                            E.mm(pC[:, 256:384], Pk[:], PTk[:], True, True)
                            E.copy(PTp[(k + 1) % 2][:], pC[:, 256:384])
                    if A1_SUB2 < 6:
                        continue
                    E.trans(pT[:, 0:64], Cmh, idb[ro:ro + 64, ro:ro + 64])
                    E.copy(CmTb[:], pT[:, 0:64])
                    E.trans(pT[ro2:ro2 + 64, 64:128], Cbh, idb[ro:ro + 64, ro:ro + 64])
                    E.copy(CbTb[ro2:ro2 + 64, :], pT[ro2:ro2 + 64, 64:128], eng="scalar")
                    E.mm(pC[ro2:ro2 + 64, 128:192], Wb[:, 0:64], CmTb[:],
                         True, True, tile_position=(0, ro2))
                    E.tt(CbPTb[ro2:ro2 + 64, :], pC[ro2:ro2 + 64, 128:192],
                         CbTb[ro2:ro2 + 64, :], ALU.add)
                    if A1_SUB2 < 7:
                        continue
                    E.mm(pD[ro:ro + 64, 0:64], CmTb[:], Wb[:, 64:128],
                         True, False, tile_position=(0, ro))
                    E.mm(pD[ro:ro + 64, 0:64], CbPTb[ro2:ro2 + 64, :], Vh,
                         False, True, tile_position=(ro2, ro))
                    if ch >= CH0:
                        E.mm(pE[ro2:ro2 + 64, 0:64], Wb[:, 0:64], Xb[:, 0:64],
                             True, False, tile_position=(0, ro2))
                        E.mm(pE[ro2:ro2 + 64, 0:64], Cbh, Qh, False, True,
                             tile_position=(ro, ro2))
                        E.tt(aTb[ro2:ro2 + 64, :], pE[ro2:ro2 + 64, 0:64],
                             mTri[ro2:ro2 + 64, :], ALU.mult)
                        E.mm(pF[ro:ro + 64, 64:128], Vh, aTb[ro2:ro2 + 64, :],
                             True, True, tile_position=(ro2, ro))
                        E.mm(pF[ro:ro + 64, 0:64], Wb[:, 64:128], Xb[:, 0:64],
                             True, False, tile_position=(0, ro))
                        E.mm(pF[ro:ro + 64, 0:64], Sbh, Qh, False, True,
                             tile_position=(ro, ro))
                        E.copy(ot1[ro:ro + 64, :], pF[ro:ro + 64, 64:128], eng="scalar")
                        gmt = half * 4 + mtl
                        E.tt(OTall[ro:ro + 64, gmt * OW + (ch - CH0) * 64:gmt * OW + (ch - CH0 + 1) * 64],
                             pF[ro:ro + 64, 0:64], ot1[ro:ro + 64, :], ALU.add)
                    E.tt(st1[ro:ro + 64, :], pD[ro:ro + 64, 0:64], Sfh, ALU.add)
                    E.ts(Sfh, st1[ro:ro + 64, :],
                         Ech[ro:ro + 64, mtl * 64 + 63:mtl * 64 + 64], ALU.mult)
                    E.copy(Sbh, Sfh, eng="scalar")
                    if A1_DBG and half == 0 and ch == A1_DBG_CH and hl == 0:
                        self.dbg_dump(fch, "dbg_fch")
                        self.dbg_dump(Lch, "dbg_Lch")
                        self.dbg_dump(Bch, "dbg_Bch")
                        self.dbg_dump(Cmch, "dbg_Cmch")
                        self.dbg_dump(Qch, "dbg_Qch")
                        self.dbg_dump(Cbch, "dbg_Cbch")
                        self.dbg_dump(M1Tb, "dbg_M1T")
                        self.dbg_dump(Xb, "dbg_Xb")
                        self.dbg_dump(Wb, "dbg_Wb")
                        self.dbg_dump(Sf[0], "dbg_Sf0")
                        self.dbg_dump(aTb, "dbg_aTb")
                        self.dbg_dump(OTall, "dbg_OT")
            self.end_scope()

        if not A1_EPI:
            self.end_scope()
            return
        self.begin_scope()
        woutb = self.sb("a1wout", [128, NDT * D], BF16)
        wo2b = self.sb("a1wo2", [64, D], BF16)
        wtmp = self.sb("a1wtmp", [128, D], F32)
        lnw = self.sb("a1lnw", [128, 8], F32)
        gout = self.sb("a1gout", [128, GW], F32)
        gsq = self.sb("a1gsq", [128, GW], F32)
        ybf = self.sb("a1ybf", [128, GW], BF16)
        rows = self.sb("a1rows", [1, 1280], F32)
        rrep = self.sb("a1rrep", [1, 256], F32)
        CH0 = A1_CH0
        E.dma("gpsimd", wo2b[0:64, :], dn["Wo2"][:])
        for mt in range(NDT):
            E.dma("sync", lnw[:, mt:mt + 1],
                  bass.AP(dn["ln_w"], mt * 128, [[1, 128], [1, 1]]))
        for kt in range(NDT):
            E.dma("sync", wtmp[:], dn["Wout"][kt * 128:(kt + 1) * 128, :])
            E.ts(woutb[:, kt * D:(kt + 1) * D], wtmp[:], lnw[:, kt:kt + 1], ALU.mult)
        if A1_DBG:
            self.dbg_dump(OTall, "dbg_OTfull")
            self.dbg_dump(y2b, "dbg_y2b")
        for ch in range(CH0, A1_NCH):
            t0 = max(ch * 64, 1365)
            cnt = (ch + 1) * 64 - t0
            i0 = 3 * (t0 - 1365) + 1
            ocol = (ch - CH0) * 64 + (t0 - ch * 64)
            for mt in range(NDT):
                E.mm(pA[:, mt * 64:mt * 64 + cnt], wo2b[0:64, mt * 128:(mt + 1) * 128],
                     ap3(y2b, 0, 64, i0, [[3, cnt]]), True, True)
            E.act(gsq[:, 0:256], pA[:, 0:256], AF.Sigmoid)
            E.act(gsq[:, 256:512], pA[:, 256:512], AF.Sigmoid)
            if A1_DBG and ch == 21:
                dbgG = self.sb("dbgG", [128, 512], F32)
                E.copy(dbgG[:], gsq[:])
                self.dbg_dump(dbgG, "dbg_gate")
            E.tt(gout[:, 0:8 * cnt],
                 ap3(OTall, 0, 128, ocol, [[OW, 8], [1, cnt]]),
                 ap3(gsq, 0, 128, 0, [[64, 8], [1, cnt]]), ALU.mult)
            E.act(gsq[:, 0:8 * cnt], gout[:, 0:8 * cnt], AF.Square)
            E.mm(pM[0:1, 0:8 * cnt], onesf[:, 0:1], gout[:, 0:8 * cnt], True, True)
            E.mm(pX[0:1, 0:8 * cnt], onesf[:, 0:1], gsq[:, 0:8 * cnt], True, True)
            E.copy(rows[0:1, 0:8 * cnt], pM[0:1, 0:8 * cnt])
            E.copy(rows[0:1, 512:512 + 8 * cnt], pX[0:1, 0:8 * cnt], eng="scalar")
            mu = rows[0:1, 1024:1024 + cnt]; ms = rows[0:1, 1088:1088 + cnt]
            vr = rows[0:1, 1152:1152 + cnt]; rr = rows[0:1, 1216:1216 + cnt]
            E.red(mu, ap3(rows, 0, 1, 0, [[1, cnt], [cnt, 8]]), ALU.add)
            E.red(ms, ap3(rows, 0, 1, 512, [[1, cnt], [cnt, 8]]), ALU.add)
            E.ts(mu, mu, 1.0 / D, ALU.mult)
            E.ts(ms, ms, 1.0 / D, ALU.mult)
            E.tt(vr, mu, mu, ALU.mult)
            E.tt(vr, ms, vr, ALU.subtract)
            E.act(vr, vr, AF.Sqrt, bias=self.fc[0:1, 0:1])
            E.recip(rr, vr)
            E.tt(mu, mu, rr, ALU.mult)
            E.copy(rrep[0:1, 0:4 * cnt], ap3(rows, 0, 1, 1216, [[0, 4], [1, cnt]]))
            E.mm(pC[:, 0:4 * cnt], onesf[0:1, :], rrep[0:1, 0:4 * cnt], True, True)
            E.copy(rrep[0:1, 0:4 * cnt], ap3(rows, 0, 1, 1024, [[0, 4], [1, cnt]]))
            E.mm(pD[:, 0:4 * cnt], onesf[0:1, :], rrep[0:1, 0:4 * cnt], True, True)
            for pp in range(2):
                hsl = slice(pp * 4 * cnt, (pp + 1) * 4 * cnt)
                psl = slice(0, 4 * cnt)
                E.tt(gout[:, hsl], gout[:, hsl], pC[:, psl], ALU.mult)
                E.tt(ybf[:, hsl], gout[:, hsl], pD[:, psl], ALU.subtract)
            for mt in range(NDT):
                for kt in range(NDT):
                    E.mm(pF[:, mt * 64:mt * 64 + cnt],
                         woutb[:, kt * D + mt * 128:kt * D + (mt + 1) * 128],
                         ybf[:, kt * cnt:(kt + 1) * cnt], kt == 0, kt == NDT - 1)
            if A1_DBG and ch == 21:
                self.dbg_dump(gsq, "dbg_gsq1")
                self.dbg_dump(gout, "dbg_gout")
                self.dbg_dump(ybf, "dbg_ybf")
                self.dbg_dump(rows, "dbg_rows")
            for mt in range(NDT):
                xap = ap3(self.xr[mt], 0, 128, i0, [[3, cnt]])
                E.tt(xap, xap, pF[:, mt * 64:mt * 64 + cnt], ALU.add)
        self.end_scope()
        self.end_scope()


_PROGS = {}


def get_prog(sections, dump=()):
    key = (tuple(sections), tuple(dump))
    if key not in _PROGS:
        p = Prog(sections, dump)
        p.load_x()
        for s in sections:
            p.sec_ln()
            getattr(p, "sec_" + s)()
        p.store_x()
        for i, dn in enumerate(dump):
            tens = {t.name: t for t in (p.xr + p.xln)}[dn]
            dt = F32 if dn.startswith("xr") else BF16
            od = p.nc.dram_tensor(f"dump{i}", list(tens.shape), dt, kind="ExternalOutput")
            p.E.dma("sync", od[:], tens[:])
        p.E.barrier()
        p.E.finalize()
        _PROGS[key] = p
    return _PROGS[key]


LAUNCH_PLAN = [["moe"], ["attn0"], ["moe"], ["ffn1"], ["attn1"], ["ffn1"]]


def kernel(**inputs):
    C = make_consts()
    x = np.ascontiguousarray(inputs["L0_attn_x"]).astype(np.float32)
    shards = [x[i] for i in range(NCORES)]
    base = {"c_" + n: np.ascontiguousarray(v) for n, v in C.items()}
    for n, v in inputs.items():
        if n != "L0_attn_x":
            base[n] = np.ascontiguousarray(v)
    plan = [sum(LAUNCH_PLAN, [])] if FUSED else LAUNCH_PLAN
    cur = shards
    for sections in plan:
        p = get_prog(sections)
        in_maps = []
        for ci in range(NCORES):
            m = {n: base[n] for n in p.inputs if n != "x_in"}
            m["x_in"] = cur[ci]
            in_maps.append(m)
        res = run_bass_kernel_spmd(p.nc, in_maps, core_ids=list(range(NCORES)))
        cur = [res.results[ci]["x_out"] for ci in range(NCORES)]
    return np.stack(cur, 0)


# version 11 (final)
# speedup vs baseline: 1.0321x; 1.0321x over previous
import numpy as np
import ml_dtypes
import concourse.bass as bass
import concourse.mybir as mybir
from concourse.bass_utils import run_bass_kernel_spmd

F32 = mybir.dt.float32
BF16 = mybir.dt.bfloat16
AF = mybir.ActivationFunctionType
ALU = mybir.AluOpType
AX = mybir.AxisListType

NCORES = 8
S = 2048
D = 1024
NG = 4
GW = 512
NDT = 8
H = 16
HD = 64
FUSED = True
A1_CH0 = 21
A1_NCH = 32
A1_EPI = True
A1_HALVES = 2
A1_SUB = 3
A1_DBG = False
A1_DBG_CH = 0
A1_SUB2 = 9
A1_BI = 4

_CONSTS = None


def make_consts():
    global _CONSTS
    if _CONSTS is not None:
        return _CONSTS
    c = {}
    c["id128"] = np.eye(128, dtype=np.float32)
    c["onesf"] = np.ones((128, 128), dtype=np.float32)
    c["negonesf"] = -np.ones((1, 128), dtype=np.float32)
    inv = 1.0 / 10000.0 ** (np.arange(0, HD, 2, dtype=np.float64) / HD)
    pos = np.arange(S, dtype=np.float64)
    ang = pos[:, None] * inv[None, :]
    cosv = np.cos(ang).T
    sinv = np.sin(ang).T
    cosT = np.zeros((128, S), np.float32)
    ssinT = np.zeros((128, S), np.float32)
    for r in range(128):
        cosT[r] = cosv[r % 32]
        ssinT[r] = (-sinv if (r % 64) < 32 else sinv)[r % 32]
    c["cosT"] = cosT
    c["ssinT"] = ssinT
    perm = np.zeros((128, 128), np.float32)
    for r in range(128):
        perm[(r // 64) * 64 + ((r % 64) + 32) % 64, r] = 1.0
    c["perm"] = perm
    caus = np.zeros((4, 128, GW), np.float32)
    for j in range(4):
        rk = np.arange(128)[:, None]
        ct = np.arange(GW)[None, :]
        caus[j] = np.where(128 * j + rk > ct, -1e9, 0.0)
    c["caus"] = np.concatenate([caus[j] for j in range(4)], axis=1)
    c["iota8"] = np.broadcast_to(np.arange(8, dtype=np.float32), (128, 8)).copy()
    c["crev8"] = 8.0 - c["iota8"]
    c["iota32"] = np.broadcast_to(np.arange(32, dtype=np.float32), (128, 32)).copy()
    c["crev32"] = 32.0 - c["iota32"]
    j0 = np.arange(128)[:, None]
    t0 = np.arange(64)[None, :]
    c["mYq"] = (j0 <= 2 * t0 + 1).astype(np.float32)
    mZ = (j0 >= 2 * t0 + 2).astype(np.float32)
    c["mZY"] = np.concatenate([c["mYq"], mZ, np.ones((128, 64), np.float32)], 1)
    c["mSU"] = (np.arange(128)[None, :] > np.arange(128)[:, None]).astype(np.float32)
    c["mSL"] = c["mSU"].T.copy()
    tri = (np.arange(64)[:, None] <= np.arange(64)[None, :]).astype(np.float32)
    c["mTri"] = np.concatenate([tri, tri], 0)
    _CONSTS = c
    return c


def _nm(x):
    t = getattr(x, "tensor", None)
    return t.name if t is not None else x.name


SEM_ROTATE = 20000


class Em:
    CE = ("tensor", "vector", "scalar", "gpsimd", "sync")
    DQ = ("sync", "gpsimd", "scalar")

    def __init__(self, nc):
        self.nc = nc
        self.q = {e: [] for e in self.CE}
        self.semidx = {}
        self.sems = {}
        self.cnt = {}
        self.active = {}
        self.NDMA = 8
        for e in self.CE:
            self._newsem("S_" + e)
        for e in self.DQ:
            for r in range(self.NDMA):
                self._newsem(f"D_{e}_{r}")
        self.dseq = {e: 0 for e in self.DQ}
        self.known = {e: {} for e in self.CE}
        self.wr = {}
        self.rd = {}

    def _newsem(self, log):
        i = self.active.get(log, -1) + 1
        h = self.nc.alloc_semaphore(f"{log}_{i}")
        self.sems.setdefault(log, []).append(h)
        self.active[log] = i
        self.cnt[log] = 0

    def _norm(self, x):
        if isinstance(x, tuple):
            return (_nm(x[0]), x[1])
        return (_nm(x), None)

    def _emit_waits(self, eng, need):
        kn = self.known[eng]
        out = []
        for key, v in sorted(need.items()):
            if kn.get(key, 0) >= v:
                continue
            log, idx = key
            out.append((self.sems[log][idx], v))
            kn[key] = v
        return out

    def _hazards(self, eng, reads, writes):
        need = {}

        def add(ent):
            s, v = ent
            if eng == "tensor" and s[0] == "S_tensor":
                return
            if need.get(s, -1) < v:
                need[s] = v
        for (t, k) in reads:
            for wk, ent in self.wr.get(t, {}).items():
                if k is None or wk is None or wk == k:
                    add(ent)
        for (t, k) in writes:
            for wk, ent in self.wr.get(t, {}).items():
                if k is None or wk is None or wk == k:
                    add(ent)
            for rk, lst in self.rd.get(t, {}).items():
                if k is None or rk is None or rk == k:
                    for ent in lst:
                        add(ent)
        return need

    def op(self, eng, build, reads=(), writes=(), dma=False):
        reads = [self._norm(r) for r in reads]
        writes = [self._norm(w) for w in writes]
        need = self._hazards(eng, reads, writes)
        if dma:
            r = self.dseq[eng] % self.NDMA
            self.dseq[eng] += 1
            log = f"D_{eng}_{r}"
            inc = 16
            prev = self.cnt[log]
            if prev > 0:
                key = (log, self.active[log])
                if need.get(key, -1) < prev:
                    need[key] = prev
        else:
            log = "S_" + eng
            inc = 1
        waits = self._emit_waits(eng, need)
        if self.cnt[log] + inc > SEM_ROTATE:
            self.barrier()
            waits = self._emit_waits(eng, self._hazards(eng, reads, writes))
            self._newsem(log)
        self.cnt[log] += inc
        ent = ((log, self.active[log]), self.cnt[log])
        self.q[eng].append((waits, build, self.sems[log][self.active[log]], inc))
        for (t, k) in writes:
            if k is None:
                self.wr[t] = {None: ent}
                self.rd.pop(t, None)
            else:
                self.wr.setdefault(t, {})[k] = ent
                if t in self.rd:
                    self.rd[t].pop(k, None)
        for (t, k) in reads:
            self.rd.setdefault(t, {}).setdefault(k, []).append(ent)

    def barrier(self):
        for e in self.CE:
            kn = self.known[e]
            need = []
            for log in self.cnt:
                c = self.cnt[log]
                if c == 0:
                    continue
                key = (log, self.active[log])
                if kn.get(key, 0) >= c:
                    continue
                need.append((self.sems[log][self.active[log]], c))
                kn[key] = c
            if need:
                self.q[e].append((need, None, None, 0))
        self.wr.clear()
        self.rd.clear()

    def rotate_fresh(self, thresh=512):
        self.barrier()
        for e in self.CE:
            log = "S_" + e
            if self.cnt[log] >= thresh:
                self._newsem(log)

    def finalize(self):
        with self.nc.Block() as blk:
            for eng in self.CE:
                ops = self.q[eng]

                def mk(ops):
                    def body(e):
                        for waits, build, sem, inc in ops:
                            for (sh, v) in waits:
                                e.wait_ge(sh, v)
                            if build is None:
                                continue
                            inst = build(e)
                            if sem is not None:
                                inst.then_inc(sem, inc)
                    return body
                getattr(blk, eng)(mk(ops))

    def dma(self, queue, out, in_, extra_reads=()):
        self.op(queue, lambda e: e.dma_start(out, in_),
                reads=[in_] + list(extra_reads), writes=[out], dma=True)

    def mm(self, out, lhsT, rhs, start, stop, reads=(), writes=(), tile_position=None):
        self.op("tensor",
                lambda e: e.matmul(out, lhsT, rhs, start=start, stop=stop,
                                   tile_position=tile_position),
                reads=[lhsT, rhs] + list(reads), writes=[out] + list(writes))

    def trans(self, out, in_, ident):
        self.op("tensor", lambda e: e.transpose(out, in_, ident),
                reads=[in_, ident], writes=[out])

    def act(self, out, in_, fn, bias=0.0, scale=1.0, reads=(), writes=()):
        br = [bias] if not isinstance(bias, float) else []
        self.op("scalar", lambda e: e.activation(out, in_, fn, bias=bias, scale=scale),
                reads=[in_] + br + list(reads), writes=[out] + list(writes))

    def tt(self, out, a, b, alu, eng="vector", reads=(), writes=()):
        self.op(eng, lambda e: e.tensor_tensor(out, a, b, alu),
                reads=[a, b] + list(reads), writes=[out] + list(writes))

    def ts(self, out, a, s1, op1, s2=None, op2=ALU.bypass, eng="vector"):
        sr = [s1] if not isinstance(s1, float) else []
        if s2 is not None and not isinstance(s2, float):
            sr.append(s2)
        self.op(eng, lambda e: e.tensor_scalar(out, a, s1, s2, op1, op2),
                reads=[a] + sr, writes=[out])

    def stt(self, out, a, s, b, op0, op1, eng="vector"):
        sr = [s] if not isinstance(s, float) else []
        self.op(eng, lambda e: e.scalar_tensor_tensor(out, a, s, b, op0, op1),
                reads=[a, b] + sr, writes=[out])

    def copy(self, out, in_, eng="vector"):
        if eng == "scalar":
            self.op(eng, lambda e: e.copy(out, in_), reads=[in_], writes=[out])
        else:
            self.op(eng, lambda e: e.tensor_copy(out, in_), reads=[in_], writes=[out])

    def memset(self, t, v, eng="vector"):
        self.op(eng, lambda e: e.memset(t, v), reads=(), writes=[t])

    def red(self, out, in_, alu, negate=False, eng="vector"):
        self.op(eng, lambda e: e.tensor_reduce(out, in_, AX.X, alu, negate=negate),
                reads=[in_], writes=[out])

    def scan(self, out, d0, d1, init, op0, op1):
        self.op("vector", lambda e: e.tensor_tensor_scan(out, d0, d1, init, op0, op1),
                reads=[d0, d1], writes=[out])

    def recip(self, out, in_):
        self.op("vector", lambda e: e.reciprocal(out, in_), reads=[in_], writes=[out])


def ap3(t, part_off, npart, col_off, dims):
    pitch = t.shape[-1]
    return bass.AP(t, part_off * pitch + col_off, [[pitch, npart]] + list(dims))


class Prog:
    def __init__(self, sections, dump=()):
        self.sections = tuple(sections)
        self.dump = tuple(dump)
        nc = bass.Bass("TRN2", target_bir_lowering=False)
        self.nc = nc
        self.inputs = set()
        self.dram = {}
        C = make_consts()
        self.E = Em(nc)
        self.d_in("x_in", [S, D], F32)
        self.dram["x_out"] = nc.dram_tensor("x_out", [S, D], F32, kind="ExternalOutput")
        self.xr = [nc.alloc_sbuf_tensor(f"xr{d}", [128, S], F32) for d in range(NDT)]
        self.xln = [nc.alloc_sbuf_tensor(f"xln{d}", [128, S], BF16) for d in range(NDT)]
        need = {"id128", "onesf"}
        if "attn0" in sections:
            need |= {"cosT", "ssinT", "perm", "caus", "iota8"}
        if "moe" in sections:
            need |= {"iota8", "crev8"}
        if "ffn1" in sections:
            need |= {"iota32", "crev32"}
        if "attn1" in sections:
            need |= {"mYq", "mZY", "mSU", "mSL", "mTri"}
        self.cs = {}
        self.cs_b16 = {}
        self.scoped_consts = {"cosT", "ssinT", "caus"}
        self.fc = nc.alloc_sbuf_tensor("fconst", [128, 4], F32)
        self.E.memset(self.fc[:, 0:1], 1e-5)
        self.E.memset(self.fc[:, 1:2], 1.0)
        self.E.memset(self.fc[:, 2:3], 1e-12)
        self.E.memset(self.fc[:, 3:4], 0.0)
        for n in sorted(need):
            arr = C[n]
            dt = self.d_in("c_" + n, list(arr.shape), F32)
            if n in self.scoped_consts:
                continue
            st = nc.alloc_sbuf_tensor("cs_" + n, list(arr.shape), F32)
            self.cs[n] = st
            self.E.dma("sync", st[:], dt[:])
            if n in ("id128", "onesf", "perm"):
                sb = nc.alloc_sbuf_tensor("cb_" + n, list(arr.shape), BF16)
                self.cs_b16[n] = sb
                self.E.dma("gpsimd", sb[:], dt[:])
        self.scopes = []
        self._uniq = {}

    def d_in(self, name, shape, dtype):
        h = self.nc.dram_tensor(name, shape, dtype, kind="ExternalInput")
        self.inputs.add(name)
        self.dram[name] = h
        return h

    def dbg_dump(self, t, name):
        od = self.nc.dram_tensor(name, list(t.shape), t.dtype, kind="ExternalOutput")
        self.E.dma("sync", od[:], t[:])

    def sb(self, name, shape, dtype):
        n = self._uniq.get(name, 0)
        self._uniq[name] = n + 1
        if n:
            name = f"{name}_{n}"
        cm = self.nc.sbuf_tensor(name, shape, dtype)
        t = cm.__enter__()
        self.scopes[-1].append(cm)
        return t

    def ps(self, name, shape, dtype=F32):
        n = self._uniq.get(name, 0)
        self._uniq[name] = n + 1
        if n:
            name = f"{name}_{n}"
        cm = self.nc.psum_tensor(name, shape, dtype)
        t = cm.__enter__()
        self.scopes[-1].append(cm)
        return t

    def begin_scope(self):
        self.scopes.append([])

    def end_scope(self):
        self.E.barrier()
        for cm in reversed(self.scopes.pop()):
            cm.__exit__(None, None, None)

    def load_x(self):
        E, nc = self.E, self.nc
        self.begin_scope()
        stg = [self.sb(f"ldstg{i}", [128, D], F32) for i in range(2)]
        pb = [self.ps(f"ldpb{i}", [128, 512], F32) for i in range(2)]
        xin = self.dram["x_in"]
        idf = self.cs["id128"]
        for tt in range(16):
            sg = stg[tt % 2]
            E.dma("sync", sg[:], xin[tt * 128:(tt + 1) * 128, :])
            for d in range(NDT):
                p = pb[d % 2]
                E.trans(p[:, 0:128], sg[:, d * 128:(d + 1) * 128], idf[:])
                E.copy(self.xr[d][:, tt * 128:(tt + 1) * 128], p[:, 0:128],
                       eng="scalar" if d % 2 else "vector")
        self.end_scope()

    def store_x(self):
        E, nc = self.E, self.nc
        self.begin_scope()
        stg = [self.sb(f"ststg{i}", [128, D], F32) for i in range(2)]
        pb = [self.ps(f"stpb{i}", [128, 512], F32) for i in range(2)]
        xout = self.dram["x_out"]
        idf = self.cs["id128"]
        for tt in range(16):
            sg = stg[tt % 2]
            for d in range(NDT):
                p = pb[d % 2]
                E.trans(p[:, 0:128], self.xr[d][:, tt * 128:(tt + 1) * 128], idf[:])
                E.copy(sg[:, d * 128:(d + 1) * 128], p[:, 0:128],
                       eng="scalar" if d % 2 else "vector")
            E.dma("sync", xout[tt * 128:(tt + 1) * 128, :], sg[:])
        self.end_scope()
        self.E.barrier()

    def sec_ln(self):
        E = self.E
        self.begin_scope()
        sq = [self.sb(f"lnsq{i}", [128, GW], F32) for i in range(2)]
        rows = self.sb("lnrows", [1, GW * 6], F32)
        pbs = self.ps("lnpbs", [128, 512], F32)
        pbq = self.ps("lnpbq", [128, 512], F32)
        pbr = self.ps("lnpbr", [128, 512], F32)
        pbm = self.ps("lnpbm", [128, 512], F32)
        onesf = self.cs["onesf"]
        t1 = [self.sb(f"lnt1{i}", [128, GW], F32) for i in range(2)]
        for g in range(NG):
            gs = slice(g * GW, (g + 1) * GW)
            for d in range(NDT):
                E.act(sq[d % 2][:], self.xr[d][:, gs], AF.Square, reads=[(self.xr[d], g)])
                E.mm(pbs[0:1, :], onesf[:, 0:1], self.xr[d][:, gs], d == 0, d == NDT - 1,
                     reads=[(self.xr[d], g)])
                E.mm(pbq[0:1, :], onesf[:, 0:1], sq[d % 2][:], d == 0, d == NDT - 1)
            mu = rows[0:1, 0:GW]
            ms = rows[0:1, GW:2 * GW]
            tt_ = rows[0:1, 2 * GW:3 * GW]
            var = rows[0:1, 3 * GW:4 * GW]
            r = rows[0:1, 4 * GW:5 * GW]
            mr = rows[0:1, 5 * GW:6 * GW]
            E.ts(mu, pbs[0:1, :], 1.0 / D, ALU.mult)
            E.ts(ms, pbq[0:1, :], 1.0 / D, ALU.mult)
            E.tt(tt_, mu, mu, ALU.mult)
            E.tt(var, ms, tt_, ALU.subtract)
            E.act(tt_, var, AF.Sqrt, bias=self.fc[0:1, 0:1])
            E.recip(r, tt_)
            E.tt(mr, mu, r, ALU.mult)
            E.mm(pbr[:, :], onesf[0:1, :], r, True, True)
            E.mm(pbm[:, :], onesf[0:1, :], mr, True, True)
            for d in range(NDT):
                E.tt(t1[d % 2][:], self.xr[d][:, gs], pbr[:, :], ALU.mult,
                     reads=[(self.xr[d], g)])
                E.tt(self.xln[d][:, gs], t1[d % 2][:], pbm[:, :], ALU.subtract,
                     writes=[(self.xln[d], g)])
        self.end_scope()


    def sec_moe(self):
        E, nc = self.E, self.nc
        w13 = self.dram.get("L0_ffn_moe_w13") or self.d_in("L0_ffn_moe_w13", [8, D, 4096], F32)
        w2 = self.dram.get("L0_ffn_moe_w2") or self.d_in("L0_ffn_moe_w2", [8, 2048, D], F32)
        rt = self.dram.get("L0_ffn_moe_router") or self.d_in("L0_ffn_moe_router", [D, 8], F32)
        self.begin_scope()
        onesf = self.cs["onesf"]
        iota8 = self.cs["iota8"]
        crev8 = self.cs["crev8"]
        id_f = self.cs["id128"]
        rsb = self.sb("rsb", [128, 64], BF16)
        E.dma("gpsimd", ap3(rsb, 0, 128, 0, [[8, 8], [1, 8]]),
              bass.AP(rt, 0, [[8, 128], [1024, 8], [1, 8]]))
        Wcall = self.sb("Wcall", [128, 128], F32)
        L = self.sb("Lrout", [128, 8], F32)
        t8a = self.sb("t8a", [128, 8], F32)
        t8b = self.sb("t8b", [128, 8], F32)
        oh1 = self.sb("oh1", [128, 8], F32)
        rwork = self.sb("rwork", [128, 8], F32)
        pb = [self.ps(f"moepb{i}", [128, 512], F32) for i in range(8)]
        m1 = rwork[:, 0:1]; m2 = rwork[:, 1:2]; ri = rwork[:, 2:3]; rib = rwork[:, 3:4]
        w1 = rwork[:, 4:5]; w2c = rwork[:, 5:6]; rd_ = rwork[:, 6:7]
        for c in range(16):
            for kt in range(NDT):
                E.mm(pb[0][:, 0:8], self.xln[kt][:, c * 128:(c + 1) * 128],
                     rsb[:, kt * 8:kt * 8 + 8], kt == 0, kt == NDT - 1,
                     reads=[(self.xln[kt], c // 4)])
            E.copy(L[:], pb[0][:, 0:8], eng="scalar")
            E.red(m1, L[:], ALU.max)
            E.ts(t8a[:], L[:], m1, ALU.is_equal)
            E.tt(t8b[:], t8a[:], crev8[:, 0:8], ALU.mult)
            E.red(ri, t8b[:], ALU.max)
            E.ts(rib, ri, -1.0, ALU.mult, 8.0, ALU.add)
            E.ts(oh1[:], iota8[:, 0:8], rib, ALU.is_equal)
            E.stt(t8a[:], oh1[:], -1e30, L[:], ALU.mult, ALU.add)
            E.red(m2, t8a[:], ALU.max)
            E.ts(t8b[:], t8a[:], m2, ALU.is_equal)
            E.tt(t8b[:], t8b[:], crev8[:, 0:8], ALU.mult)
            E.red(ri, t8b[:], ALU.max)
            E.ts(rib, ri, -1.0, ALU.mult, 8.0, ALU.add)
            E.ts(t8b[:], iota8[:, 0:8], rib, ALU.is_equal)
            E.tt(rd_, m1, m2, ALU.subtract)
            E.act(w1, rd_, AF.Sigmoid)
            E.ts(w2c, w1, -1.0, ALU.mult, 1.0, ALU.add)
            E.ts(t8a[:], oh1[:], w1, ALU.mult)
            E.stt(Wcall[:, c * 8:(c + 1) * 8], t8b[:], w2c, t8a[:], ALU.mult, ALU.add)
        HC = 256
        wgb = [self.sb(f"wgb{i}", [128, 2048], BF16) for i in range(2)]
        wub = [self.sb(f"wub{i}", [128, 2048], BF16) for i in range(2)]
        w2b = [self.sb(f"w2b{i}", [128, 512], BF16) for i in range(2)]
        wBs = [self.sb(f"wBs{i}", [128, GW], F32) for i in range(2)]
        Dg = self.sb("Dg", [128, 128], F32)
        sg = [self.sb(f"moesg{i}", [128, GW], BF16) for i in range(2)]
        hu = [self.sb(f"moehu{i}", [128, GW], BF16) for i in range(2)]
        hpg = [self.sb(f"hpg{i}", [128, 8192], BF16) for i in range(2)]
        macc = [self.sb(f"macc{i}", [128, 4096], BF16) for i in range(2)]

        def dma_w13(e, hc, which, buf):
            coff = which * 2048 + hc * HC
            E.dma("gpsimd", ap3(buf, 0, 128, 0, [[HC, 8], [1, HC]]),
                  bass.AP(w13, e * D * 4096 + coff, [[4096, 128], [128 * 4096, 8], [1, HC]]))

        def dma_w2(e, dh, kt, buf):
            E.dma("gpsimd", buf[:],
                  bass.AP(w2, e * 2048 * D + kt * 128 * D + dh * 512, [[1024, 128], [1, 512]]))

        nhc = 2048 // HC
        for pair in range(2):
            gs = (2 * pair, 2 * pair + 1)
            for e in range(8):
                for gi, g in enumerate(gs):
                    for cc in range(4):
                        c = g * 4 + cc
                        E.ts(Dg[:], id_f[:], Wcall[:, c * 8 + e:c * 8 + e + 1], ALU.mult)
                        E.mm(pb[7][:, cc * 128:(cc + 1) * 128], onesf[:], Dg[:], True, True)
                    E.copy(wBs[gi][:], pb[7][:], eng="vector")
                dma_w13(e, 0, 0, wgb[0])
                dma_w13(e, 0, 1, wub[0])
                for hc in range(nhc):
                    wg, wu = wgb[hc % 2], wub[hc % 2]
                    if hc + 1 < nhc:
                        dma_w13(e, hc + 1, 0, wgb[(hc + 1) % 2])
                        dma_w13(e, hc + 1, 1, wub[(hc + 1) % 2])
                    for gi, g in enumerate(gs):
                        gsl = slice(g * GW, (g + 1) * GW)
                        for mt in range(2):
                            for kt in range(NDT):
                                E.mm(pb[mt][:], wg[:, kt * HC + mt * 128:kt * HC + (mt + 1) * 128],
                                     self.xln[kt][:, gsl], kt == 0, kt == NDT - 1,
                                     reads=[(self.xln[kt], g)])
                                E.mm(pb[2 + mt][:], wu[:, kt * HC + mt * 128:kt * HC + (mt + 1) * 128],
                                     self.xln[kt][:, gsl], kt == 0, kt == NDT - 1,
                                     reads=[(self.xln[kt], g)])
                        for mt in range(2):
                            E.act(sg[mt][:], pb[mt][:], AF.Silu)
                            E.tt(hu[mt][:], sg[mt][:], pb[2 + mt][:], ALU.mult)
                            E.tt(hpg[gi][:, (hc * 2 + mt) * GW:(hc * 2 + mt + 1) * GW],
                                 hu[mt][:], wBs[gi][:], ALU.mult)
                for dh in range(2):
                    dma_w2(e, dh, 0, w2b[0])
                    for kt in range(16):
                        if kt + 1 < 16:
                            dma_w2(e, dh, kt + 1, w2b[(kt + 1) % 2])
                        for gi in range(2):
                            for mt in range(4):
                                E.mm(pb[gi * 4 + mt][:],
                                     w2b[kt % 2][:, mt * 128:(mt + 1) * 128],
                                     hpg[gi][:, kt * GW:(kt + 1) * GW],
                                     kt == 0, kt == 15)
                    for gi in range(2):
                        for mt in range(4):
                            dst = macc[gi][:, (dh * 4 + mt) * GW:(dh * 4 + mt + 1) * GW]
                            if e == 0:
                                E.copy(dst, pb[gi * 4 + mt][:], eng="vector")
                            else:
                                E.tt(dst, dst, pb[gi * 4 + mt][:], ALU.add)
            for gi, g in enumerate(gs):
                gsl = slice(g * GW, (g + 1) * GW)
                for d in range(NDT):
                    E.stt(self.xr[d][:, gsl], macc[gi][:, d * GW:(d + 1) * GW], 0.5,
                          self.xr[d][:, gsl], ALU.mult, ALU.add)
        self.end_scope()


    def sec_attn0(self):
        E, nc = self.E, self.nc
        Wq = self.dram.get("L0_attn_Wq") or self.d_in("L0_attn_Wq", [D, D], F32)
        bq = self.dram.get("L0_attn_bq") or self.d_in("L0_attn_bq", [D], F32)
        Wkv = self.dram.get("L0_attn_Wkv") or self.d_in("L0_attn_Wkv", [D, 2048], F32)
        bkv = self.dram.get("L0_attn_bkv") or self.d_in("L0_attn_bkv", [2048], F32)
        Wo = self.dram.get("L0_attn_Wo") or self.d_in("L0_attn_Wo", [D, D], F32)
        bo = self.dram.get("L0_attn_bo") or self.d_in("L0_attn_bo", [D], F32)
        self.begin_scope()
        onesf = self.cs["onesf"]
        onesb = self.cs_b16["onesf"]
        idb = self.cs_b16["id128"]
        permb = self.cs_b16["perm"]
        C = make_consts()
        cosT = self.sb("a0cosT", [128, S], F32)
        ssinT = self.sb("a0ssinT", [128, S], F32)
        caus = self.sb("a0caus", list(C["caus"].shape), F32)
        E.dma("sync", cosT[:], self.dram["c_cosT"][:])
        E.dma("sync", ssinT[:], self.dram["c_ssinT"][:])
        E.dma("sync", caus[:], self.dram["c_caus"][:])
        VW = 520
        qh = self.sb("a0q", [128, 4 * S], BF16)
        kh = self.sb("a0k", [128, 4 * S], BF16)
        vTh = self.sb("a0vT", [128, 16 * VW], BF16)
        vstage = self.sb("a0vs", [128, GW], BF16)
        tmpA = self.sb("a0tmpA", [128, S], BF16)
        tmpB = [self.sb("a0tmpB0", [128, GW], BF16)] * 2
        expb = [self.sb(f"a0exp{i}", [128, GW], BF16) for i in range(2)]
        mskt = self.sb("a0mskt", [128, GW], F32)
        negc = self.sb("a0negc", [128, GW], BF16)
        qn = self.sb("a0qn", [1, GW], F32)
        nrow = self.sb("a0nrow", [1, GW], F32)
        kmx = self.sb("a0kmx", [1, 64], F32)
        rdenb = self.sb("a0rden", [64, GW], F32)
        oblk = self.sb("a0oblk", [128, 4 * GW], BF16)
        brow = self.sb("a0brow", [1, 4096], BF16)
        onesw = self.sb("a0onesw", [1, GW], BF16)
        wbuf = [self.sb(f"a0wb{i}", [128, GW], BF16) for i in range(2)]
        wob = wbuf
        pb = [self.ps(f"a0pb{i}", [128, 512], F32) for i in range(5)]
        pbT = self.ps("a0pbT", [128, 1024], BF16)
        E.memset(onesw[:], 1.0)
        E.dma("gpsimd", brow[0:1, 0:1024], bass.AP(bq, 0, [[1024, 1], [1, 1024]]))
        E.dma("gpsimd", brow[0:1, 1024:3072], bass.AP(bkv, 0, [[2048, 1], [1, 2048]]))
        E.dma("gpsimd", brow[0:1, 3072:4096], bass.AP(bo, 0, [[1024, 1], [1, 1024]]))
        E.memset(ap3(vTh, 0, 128, 64, [[VW, 16], [65, 8]]), 1.0)

        def proj_pass(half, wdram, wcol, dst, bias_off, scale):
            for g in range(NG):
                gsl = slice(g * GW, (g + 1) * GW)
                for kt in range(NDT):
                    wb = wbuf[kt % 2]
                    E.dma("gpsimd", wb[:],
                          bass.AP(wdram, kt * 128 * wdram.shape[1] + wcol,
                                  [[wdram.shape[1], 128], [1, 512]]))
                    for mt in range(4):
                        E.mm(pb[mt][:], wb[:, mt * 128:(mt + 1) * 128],
                             self.xln[kt][:, gsl], kt == 0, False,
                             reads=[(self.xln[kt], g)])
                for mt in range(4):
                    E.mm(pb[mt][:], brow[0:1, bias_off + mt * 128:bias_off + (mt + 1) * 128],
                         onesw[0:1, :], False, True)
                    E.act(dst[:, mt * S + g * GW:mt * S + (g + 1) * GW], pb[mt][:],
                          AF.Identity, scale=scale)

        def rope(dst):
            for mt in range(4):
                tsl = slice(mt * S, (mt + 1) * S)
                E.tt(tmpA[:], dst[:, tsl], cosT[:], ALU.mult)
                for gg in range(4):
                    E.mm(pb[gg][:], permb[:], dst[:, mt * S + gg * GW:mt * S + (gg + 1) * GW],
                         True, True)
                for gg in range(4):
                    E.tt(tmpB[gg % 2][:], pb[gg][:], ssinT[:, gg * GW:(gg + 1) * GW], ALU.mult)
                    E.tt(dst[:, mt * S + gg * GW:mt * S + (gg + 1) * GW],
                         tmpA[:, gg * GW:(gg + 1) * GW], tmpB[gg % 2][:], ALU.add)

        for half in range(2):
            proj_pass(half, Wq, half * 512, qh, half * 512, 0.125)
            proj_pass(half, Wkv, half * 512, kh, 1024 + half * 512, 1.0)
            rope(qh)
            rope(kh)
            for g in range(NG):
                gsl = slice(g * GW, (g + 1) * GW)
                for kt in range(NDT):
                    wb = wbuf[kt % 2]
                    E.dma("gpsimd", wb[:],
                          bass.AP(Wkv, kt * 128 * 2048 + 1024 + half * 512,
                                  [[2048, 128], [1, 512]]))
                    for mt in range(4):
                        E.mm(pb[mt][:], wb[:, mt * 128:(mt + 1) * 128],
                             self.xln[kt][:, gsl], kt == 0, False,
                             reads=[(self.xln[kt], g)])
                for mt in range(4):
                    E.mm(pb[mt][:], brow[0:1, 2048 + half * 512 + mt * 128:2048 + half * 512 + (mt + 1) * 128],
                         onesw[0:1, :], False, True)
                    E.act(vstage[:], pb[mt][:], AF.Identity)
                    for b in range(4):
                        E.trans(pbT[:, b * 128:(b + 1) * 128], vstage[:, b * 128:(b + 1) * 128],
                                idb[:])
                    for b in range(4):
                        ti = 4 * g + b
                        E.copy(vTh[:, ti * VW + (2 * mt) * 65:ti * VW + (2 * mt) * 65 + 64],
                               pbT[:, b * 128:b * 128 + 64], eng="vector")
                        E.copy(vTh[:, ti * VW + (2 * mt + 1) * 65:ti * VW + (2 * mt + 1) * 65 + 64],
                               pbT[:, b * 128 + 64:(b + 1) * 128], eng="scalar")
            for hl in range(8):
                mt, ro = hl // 2, 64 * (hl % 2)
                for gg in range(4):
                    E.act(tmpA[ro:ro + 64, 0:GW],
                          kh[ro:ro + 64, mt * S + gg * GW:mt * S + (gg + 1) * GW],
                          AF.Square)
                    E.mm(pb[4][0:1, :], onesb[ro:ro + 64, 0:1],
                         tmpA[ro:ro + 64, 0:GW], True, True)
                    E.red(kmx[0:1, hl * 4 + gg:hl * 4 + gg + 1], pb[4][0:1, :], ALU.max)
                E.red(kmx[0:1, 32 + hl:32 + hl + 1], kmx[0:1, hl * 4:hl * 4 + 4], ALU.max)
                E.act(kmx[0:1, 48 + hl:48 + hl + 1], kmx[0:1, 32 + hl:32 + hl + 1], AF.Sqrt)
            for g in range(NG):
                for hl in range(8):
                    mt, ro = hl // 2, 64 * (hl % 2)
                    E.act(tmpA[ro:ro + 64, 0:GW],
                          qh[ro:ro + 64, mt * S + g * GW:mt * S + (g + 1) * GW], AF.Square)
                    E.mm(pb[4][0:1, :], onesb[ro:ro + 64, 0:1],
                         tmpA[ro:ro + 64, 0:GW], True, True)
                    E.act(qn[:], pb[4][0:1, :], AF.Sqrt)
                    E.ts(negc[ro:ro + 1, :], qn[:], kmx[0:1, 48 + hl:48 + hl + 1], ALU.mult,
                         -1.0, ALU.mult)
                    ni = 4 * g + 4
                    for i in range(ni):
                        sb_ = pb[i % 2]
                        E.mm(sb_[:], kh[ro:ro + 64, mt * S + i * 128:mt * S + (i + 1) * 128],
                             qh[ro:ro + 64, mt * S + g * GW:mt * S + (g + 1) * GW], True, False)
                        E.mm(sb_[:], onesb[ro:ro + 1, 0:128], negc[ro:ro + 1, :], False, True)
                        if i >= 4 * g:
                            j = i - 4 * g
                            E.tt(mskt[:], sb_[:], caus[:, j * GW:(j + 1) * GW], ALU.add)
                            E.act(expb[i % 2][:], mskt[:], AF.Exp)
                        else:
                            E.act(expb[i % 2][:], sb_[:], AF.Exp)
                        E.mm(pb[2][0:65, :], vTh[:, i * VW + hl * 65:i * VW + (hl + 1) * 65],
                             expb[i % 2][:], i == 0, i == ni - 1)
                    E.recip(nrow[:], pb[2][64:65, :])
                    E.mm(pb[3][0:64, :], onesf[0:1, 0:64], nrow[:], True, True)
                    E.copy(rdenb[:], pb[3][0:64, :], eng="scalar")
                    E.tt(oblk[ro:ro + 64, mt * GW:(mt + 1) * GW],
                         pb[2][0:64, :], rdenb[:], ALU.mult)
                for dh in range(2):
                    for kt in range(4):
                        wb = wob[kt % 2]
                        E.dma("gpsimd", wb[:],
                              bass.AP(Wo, (half * 512 + kt * 128) * D + dh * 512,
                                      [[D, 128], [1, 512]]))
                        for mt in range(4):
                            E.mm(pb[mt][:], wb[:, mt * 128:(mt + 1) * 128],
                                 oblk[:, kt * GW:(kt + 1) * GW], kt == 0,
                                 kt == 3 and half == 1)
                    for mt in range(4):
                        if half == 0:
                            E.mm(pb[mt][:], brow[0:1, 3072 + dh * 512 + mt * 128:3072 + dh * 512 + (mt + 1) * 128],
                                 onesw[0:1, :], False, True)
                        d = dh * 4 + mt
                        gsl = slice(g * GW, (g + 1) * GW)
                        E.tt(self.xr[d][:, gsl], self.xr[d][:, gsl], pb[mt][:], ALU.add)
        self.end_scope()


    def sec_ffn1(self):
        E, nc = self.E, self.nc
        rw = self.dram.get("L1_ffn_router_w") or self.d_in("L1_ffn_router_w", [32, D], F32)
        rb = self.dram.get("L1_ffn_router_b") or self.d_in("L1_ffn_router_b", [32], F32)
        w1 = self.dram.get("L1_ffn_weights1") or self.d_in("L1_ffn_weights1", [32, D, 256], F32)
        b1 = self.dram.get("L1_ffn_bias1") or self.d_in("L1_ffn_bias1", [32, 256], F32)
        w2 = self.dram.get("L1_ffn_weights2") or self.d_in("L1_ffn_weights2", [32, 256, D], F32)
        b2 = self.dram.get("L1_ffn_bias2") or self.d_in("L1_ffn_bias2", [32, D], F32)
        self.begin_scope()
        onesb = self.cs_b16["onesf"]
        idb = self.cs_b16["id128"]
        idf = self.cs["id128"]
        iota32 = self.cs["iota32"]
        crev32 = self.cs["crev32"]
        ohcall = self.sb("f1ohc", [128, 512], F32)
        OHem = self.sb("f1ohem", [32, S], BF16)
        ohBs = self.sb("f1ohbs", [128, GW], BF16)
        b2sb = self.sb("f1b2", [32, D], BF16)
        routerT = self.sb("f1rT", [128, 256], BF16)
        onesw = self.sb("f1ones", [1, GW], BF16)
        L32 = self.sb("f1L", [128, 32], F32)
        t32a = self.sb("f1ta", [128, 32], F32)
        rwk = self.sb("f1rw", [128, 4], F32)
        ohb = self.sb("f1ohb", [128, 32], BF16)
        rbrow = self.sb("f1rb", [1, 32], BF16)
        NEB = 4
        pb = [self.ps(f"f1pb{i}", [128, 512], F32) for i in range(6)]
        pbT = self.ps("f1pbT", [128, 1024], BF16)
        E.memset(onesw[:], 1.0)
        E.dma("gpsimd", b2sb[:], b2[:])
        E.dma("gpsimd", rbrow[0:1, :], bass.AP(rb, 0, [[32, 1], [1, 32]]))
        self.begin_scope()
        rsb = self.sb("f1rsb", [32, D], F32)
        E.dma("sync", rsb[:], rw[:])
        for kt in range(NDT):
            E.trans(pb[5][0:128, 0:32], rsb[0:32, kt * 128:(kt + 1) * 128], idf[0:32, 0:32])
            E.copy(routerT[:, kt * 32:(kt + 1) * 32], pb[5][0:128, 0:32], eng="scalar")
        self.end_scope()
        m1 = rwk[:, 0:1]; ri = rwk[:, 1:2]; rib = rwk[:, 2:3]
        for c in range(16):
            for kt in range(NDT):
                E.mm(pb[5][:, 0:32], self.xln[kt][:, c * 128:(c + 1) * 128],
                     routerT[:, kt * 32:(kt + 1) * 32], kt == 0, False,
                     reads=[(self.xln[kt], c // 4)])
            E.mm(pb[5][:, 0:32], onesb[0:1, 0:128], rbrow[0:1, :], False, True)
            E.copy(L32[:], pb[5][:, 0:32], eng="scalar")
            E.red(m1, L32[:], ALU.max)
            E.ts(t32a[:], L32[:], m1, ALU.is_equal)
            E.tt(t32a[:], t32a[:], crev32[:], ALU.mult)
            E.red(ri, t32a[:], ALU.max)
            E.ts(rib, ri, -1.0, ALU.mult, 32.0, ALU.add)
            E.ts(ohcall[:, c * 32:(c + 1) * 32], iota32[:], rib, ALU.is_equal)
            E.copy(ohb[:], ohcall[:, c * 32:(c + 1) * 32])
            E.trans(pbT[0:32, 0:128], ohb[:], idb[:])
            E.copy(OHem[:, c * 128:(c + 1) * 128], pbT[0:32, 0:128], eng="scalar")
        hb = [self.sb(f"f1hb{i}", [128, 2 * GW], BF16) for i in range(4)]
        b1row = [self.sb("f1b1r0", [1, 256], BF16)] * 2
        w1blk = self.sb("f1w1", [128, NEB * 2048], BF16)
        w2blk = self.sb("f1w2", [128, NEB * 2048], BF16)
        zacc = self.sb("f1z", [128, NDT * S], BF16)
        Dg = self.sb("f1dg", [128, 128], F32)
        for eb in range(32 // NEB):
            for i in range(NEB):
                e = eb * NEB + i
                E.dma("gpsimd", ap3(w1blk, 0, 128, i * 2048, [[256, 8], [1, 256]]),
                      bass.AP(w1, e * D * 256, [[256, 128], [128 * 256, 8], [1, 256]]))
                E.dma("gpsimd", ap3(w2blk, 0, 128, i * 2048, [[1024, 2], [1, 1024]]),
                      bass.AP(w2, e * 256 * D, [[1024, 128], [128 * 1024, 2], [1, 1024]]))
            for g in range(NG):
                gsl = slice(g * GW, (g + 1) * GW)
                for i in range(NEB):
                    e = eb * NEB + i
                    E.dma("gpsimd", b1row[i % 2][0:1, :],
                          bass.AP(b1, e * 256, [[256, 1], [1, 256]]))
                    for mt in range(2):
                        hbank = pb[0] if mt == 0 else pb[5]
                        for kt in range(NDT):
                            E.mm(hbank[:],
                                 w1blk[:, i * 2048 + kt * 256 + mt * 128:i * 2048 + kt * 256 + (mt + 1) * 128],
                                 self.xln[kt][:, gsl], kt == 0, False,
                                 reads=[(self.xln[kt], g)])
                        E.mm(hbank[:],
                             b1row[i % 2][0:1, mt * 128:(mt + 1) * 128],
                             onesw[0:1, :], False, True)
                        E.act(hb[i][:, mt * GW:(mt + 1) * GW], hbank[:], AF.Relu)
                    for cc in range(4):
                        c = g * 4 + cc
                        E.ts(Dg[:], idf[:], ohcall[:, c * 32 + e:c * 32 + e + 1], ALU.mult)
                        E.mm(pb[5][:, cc * 128:(cc + 1) * 128], self.cs["onesf"][:],
                             Dg[:], True, True)
                    E.copy(ohBs[:], pb[5][:])
                    for mt in range(2):
                        E.tt(hb[i][:, mt * GW:(mt + 1) * GW],
                             hb[i][:, mt * GW:(mt + 1) * GW], ohBs[:], ALU.mult)
                for dh in range(2):
                    for dt in range(4):
                        dd = dh * 4 + dt
                        if eb == 0:
                            E.mm(pb[1 + dt][:], b2sb[0:32, dd * 128:(dd + 1) * 128],
                                 OHem[0:32, gsl], True, False)
                        for i in range(NEB):
                            for kt in range(2):
                                E.mm(pb[1 + dt][:],
                                     w2blk[:, i * 2048 + kt * 1024 + dd * 128:i * 2048 + kt * 1024 + (dd + 1) * 128],
                                     hb[i][:, kt * GW:(kt + 1) * GW],
                                     eb != 0 and i == 0 and kt == 0,
                                     i == NEB - 1 and kt == 1)
                        dst = zacc[:, dd * S + g * GW:dd * S + (g + 1) * GW]
                        if eb == 0:
                            E.copy(dst, pb[1 + dt][:])
                        else:
                            E.tt(dst, dst, pb[1 + dt][:], ALU.add)
        for d in range(NDT):
            for g in range(NG):
                gsl = slice(g * GW, (g + 1) * GW)
                E.act(ohBs[:], zacc[:, d * S + g * GW:d * S + (g + 1) * GW], AF.Relu)
                E.stt(self.xr[d][:, gsl], ohBs[:], 0.5, self.xr[d][:, gsl],
                      ALU.mult, ALU.add)
        self.end_scope()


    def sec_attn1(self):
        E, nc = self.E, self.nc
        dn = {}
        for nm, shp in (("Wq", [D, D]), ("Wk", [D, D]), ("Wv", [D, D]),
                        ("Wf1", [D, 64]), ("Wf2", [64, D]), ("Wg", [D, 16]),
                        ("Wo1", [D, 64]), ("Wo2", [64, D]), ("ln_w", [D]),
                        ("Wout", [D, D])):
            key = "L1_attn_" + nm
            dn[nm] = self.dram.get(key) or self.d_in(key, shp, F32)
        self.begin_scope()
        onesf = self.cs["onesf"]
        onesb = self.cs_b16["onesf"]
        idb = self.cs_b16["id128"]
        idf = self.cs["id128"]
        mYq = self.cs["mYq"]
        mZY = self.cs["mZY"]
        mSU = self.cs["mSU"]
        mSL = self.cs["mSL"]
        mTri = self.cs["mTri"]
        CH0 = A1_CH0
        NCH = A1_NCH
        OW = max(1, NCH - CH0) * 64
        Sf = [self.sb(f"a1Sf{i}", [128, 64], F32) for i in range(4)]
        Sb = [self.sb(f"a1Sb{i}", [128, 64], BF16) for i in range(4)]
        OTall = self.sb("a1OT", [128, NDT * OW], BF16)
        y1b = self.sb("a1y1", [64, S], BF16)
        y2b = self.sb("a1y2", [64, S], BF16)
        gtok = self.sb("a1gtok", [128, 256], F32)
        wsm = self.sb("a1wsm", [128, 512 + 128], BF16)
        pA = self.ps("a1pA", [128, 512], F32)
        pM = self.ps("a1pM", [128, 512], F32)
        pX = self.ps("a1pX", [128, 512], F32)
        pC = self.ps("a1pC", [128, 512], F32)
        pD = self.ps("a1pD", [128, 512], F32)
        pE = self.ps("a1pE", [128, 512], F32)
        pF = self.ps("a1pF", [128, 512], F32)
        pT = self.ps("a1pT", [128, 1024], BF16)
        E.dma("gpsimd", ap3(wsm, 0, 128, 0, [[64, 8], [1, 64]]),
              bass.AP(dn["Wf1"], 0, [[64, 128], [128 * 64, 8], [1, 64]]))
        E.dma("gpsimd", ap3(wsm, 0, 128, 512, [[16, 8], [1, 16]]),
              bass.AP(dn["Wg"], 0, [[16, 128], [128 * 16, 8], [1, 16]]))
        self.begin_scope()
        wo1b = self.sb("a1wo1", [128, 512], BF16)
        gml = self.sb("a1gml", [16, S], F32)
        E.dma("gpsimd", ap3(wo1b, 0, 128, 0, [[64, 8], [1, 64]]),
              bass.AP(dn["Wo1"], 0, [[64, 128], [128 * 64, 8], [1, 64]]))
        for g in range(NG):
            if A1_SUB < 1:
                break
            gsl = slice(g * GW, (g + 1) * GW)
            for kt in range(NDT):
                E.mm(pA[0:64, :], wsm[:, kt * 64:(kt + 1) * 64],
                     self.xln[kt][:, gsl], kt == 0, kt == NDT - 1,
                     reads=[(self.xln[kt], g)])
                E.mm(pM[0:64, :], wo1b[:, kt * 64:(kt + 1) * 64],
                     self.xln[kt][:, gsl], kt == 0, kt == NDT - 1,
                     reads=[(self.xln[kt], g)])
                E.mm(pX[0:16, :], wsm[:, 512 + kt * 16:512 + (kt + 1) * 16],
                     self.xln[kt][:, gsl], kt == 0, kt == NDT - 1,
                     reads=[(self.xln[kt], g)])
            if A1_SUB < 2:
                continue
            E.act(y1b[0:64, gsl], pA[0:64, :], AF.Identity)
            E.act(y2b[0:64, gsl], pM[0:64, :], AF.Identity)
            E.act(gml[0:16, gsl], pX[0:16, :], AF.Sigmoid)
        for tp in range(16):
            if A1_SUB < 3:
                break
            E.trans(pF[0:128, 0:16], gml[0:16, tp * 128:(tp + 1) * 128], idf[0:16, 0:16])
            E.copy(gtok[:, tp * 16:(tp + 1) * 16], pF[0:128, 0:16], eng="scalar")
        self.end_scope()

        for half in range(A1_HALVES):
            self.begin_scope()
            qs = self.sb("a1qs", [128, 4 * S], BF16)
            ks = self.sb("a1ks", [128, 4 * S], BF16)
            vTs = self.sb("a1vTs", [128, 16 * 512], BF16)
            vstage = self.sb("a1vst", [128, GW], BF16)
            wbuf = [self.sb(f"a1wb{i}", [128, GW], BF16) for i in range(2)]
            wf2b = self.sb("a1wf2", [64, 512], BF16)
            nrow = self.sb("a1nrow", [1, GW], F32)
            sqb = self.sb("a1sqb", [128, GW], BF16)
            E.dma("gpsimd", wf2b[0:64, :],
                  bass.AP(dn["Wf2"], half * 512, [[D, 64], [1, 512]]))

            def proj(wdram, dst):
                for g in range(NG):
                    gsl = slice(g * GW, (g + 1) * GW)
                    for kt in range(NDT):
                        wb = wbuf[kt % 2]
                        E.dma("gpsimd", wb[:],
                              bass.AP(wdram, kt * 128 * D + half * 512, [[D, 128], [1, 512]]))
                        for mt in range(4):
                            E.mm(pb4[mt][:], wb[:, mt * 128:(mt + 1) * 128],
                                 self.xln[kt][:, gsl], kt == 0, kt == NDT - 1,
                                 reads=[(self.xln[kt], g)])
                    for mt in range(4):
                        if dst is None:
                            E.act(vstage[:], pb4[mt][:], AF.Silu)
                            for b in range(4):
                                E.trans(pT[:, b * 128:(b + 1) * 128],
                                        vstage[:, b * 128:(b + 1) * 128], idb[:])
                            for b in range(4):
                                ti = 4 * g + b
                                E.copy(vTs[:, ti * 512 + (2 * mt) * 64:ti * 512 + (2 * mt) * 64 + 64],
                                       pT[:, b * 128:b * 128 + 64], eng="vector")
                                E.copy(vTs[:, ti * 512 + (2 * mt + 1) * 64:ti * 512 + (2 * mt + 1) * 64 + 64],
                                       pT[:, b * 128 + 64:(b + 1) * 128], eng="scalar")
                        else:
                            E.act(dst[:, mt * S + g * GW:mt * S + (g + 1) * GW],
                                  pb4[mt][:], AF.Silu)

            pb4 = [pA, pM, pX, pE]
            proj(dn["Wq"], qs)
            proj(dn["Wk"], ks)
            proj(dn["Wv"], None)
            for hl in range(8):
                mt, ro = hl // 2, 64 * (hl % 2)
                for gg in range(4):
                    E.act(sqb[ro:ro + 64, :],
                          ks[ro:ro + 64, mt * S + gg * GW:mt * S + (gg + 1) * GW], AF.Square)
                    E.mm(pC[0:1, :], onesb[ro:ro + 64, 0:1], sqb[ro:ro + 64, :], True, True)
                    E.act(nrow[:], pC[0:1, :], AF.Sqrt)
                    E.ts(nrow[:], nrow[:], 1e-12, ALU.max)
                    E.recip(nrow[:], nrow[:])
                    E.mm(pD[ro:ro + 64, :], onesf[0:1, 0:64], nrow[:], True, True)
                    E.tt(ks[ro:ro + 64, mt * S + gg * GW:mt * S + (gg + 1) * GW],
                         ks[ro:ro + 64, mt * S + gg * GW:mt * S + (gg + 1) * GW],
                         pD[ro:ro + 64, :], ALU.mult)
            if A1_DBG and half == 0:
                self.dbg_dump(qs, "dbg_qs")
                self.dbg_dump(ks, "dbg_ks")
                self.dbg_dump(vTs, "dbg_vTs")
                self.dbg_dump(y1b, "dbg_y1b")
                self.dbg_dump(gtok, "dbg_gtok")
            for i in range(4):
                E.memset(Sf[i][:], 0.0)
                E.memset(Sb[i][:], 0.0)
            fch = self.sb("a1fch", [128, 256], F32)
            Lch = self.sb("a1Lch", [128, 256], F32)
            Einv = self.sb("a1Einv", [128, 256], F32)
            Epch = self.sb("a1Epch", [128, 256], F32)
            Epinv = ft2x = None
            ft1 = self.sb("a1ft1", [128, 256], F32)
            ft2 = self.sb("a1ft2", [128, 256], F32)
            Epinv = ft2
            Ech = fch
            onec = self.sb("a1onec", [128, 64], F32)
            kgb = self.sb("a1kgb", [128, 256], BF16)
            Bch = self.sb("a1Bch", [128, 512], BF16)
            Cmch = self.sb("a1Cmch", [128, 512], BF16)
            Qch = self.sb("a1Qch", [128, 256], BF16)
            Cbch = self.sb("a1Cbch", [128, 256], BF16)
            Dg2 = self.sb("a1Dg2", [128, 64], F32)
            M1Tb = self.sb("a1M1T", [128, 128], BF16)
            M1b = self.sb("a1M1", [128, 128], BF16)
            PTp = [self.sb(f"a1PT{i}", [128, 128], BF16) for i in range(2)]
            Pp = [self.sb(f"a1P{i}", [128, 128], BF16) for i in range(2)]
            Xb = self.sb("a1Xb", [128, 192], BF16)
            Wb = self.sb("a1Wb", [128, 128], BF16)
            CmTb = self.sb("a1CmT", [128, 64], BF16)
            CbTb = self.sb("a1CbT", [128, 64], BF16)
            CbPTb = self.sb("a1CbPT", [128, 64], BF16)
            aTb = self.sb("a1aTb", [128, 64], BF16)
            ot1 = self.sb("a1ot1", [128, 64], F32)
            st1 = self.sb("a1st1", [128, 64], F32)
            E.memset(onec[:], 1.0)
            qs3 = lambda c0: ap3(qs, 0, 128, c0, [[S, 4], [1, 64]])
            ks3 = lambda c0: ap3(ks, 0, 128, c0, [[S, 4], [1, 64]])
            for ch in range(NCH):
                ro2 = 64 * (ch % 2)
                c0 = ch * 64
                for mtl in range(4):
                    E.mm(pA[:, mtl * 64:(mtl + 1) * 64],
                         wf2b[0:64, mtl * 128:(mtl + 1) * 128],
                         y1b[0:64, c0:c0 + 64], mtl == mtl, True)
                E.act(ft1[:], pA[:, 0:256], AF.Abs)
                E.act(ft1[:], ft1[:], AF.Exp, scale=-1.0)
                E.act(ft1[:], ft1[:], AF.Ln, bias=self.fc[:, 1:2])
                E.ts(ft2[:], pA[:, 0:256], 0.0, ALU.min)
                E.tt(fch[:], ft2[:], ft1[:], ALU.subtract)
                for mtl in range(4):
                    E.scan(Lch[:, mtl * 64:(mtl + 1) * 64], onec[:],
                           fch[:, mtl * 64:(mtl + 1) * 64], 0.0, ALU.mult, ALU.add)
                E.tt(ft1[:], Lch[:], fch[:], ALU.subtract)
                E.act(Ech[:], Lch[:], AF.Exp)
                E.act(Einv[:], Lch[:], AF.Exp, scale=-1.0)
                E.act(Epch[:], ft1[:], AF.Exp)
                E.act(Epinv[:], ft1[:], AF.Exp, scale=-1.0)
                for hl in range(8 if A1_SUB2 >= 2 else 0):
                    mtl, ro = hl // 2, 64 * (hl % 2)
                    gcol = gtok[ro2:ro2 + 64, (ch // 2) * 16 + half * 8 + hl:(ch // 2) * 16 + half * 8 + hl + 1]
                    E.ts(Dg2[ro2:ro2 + 64, :], idf[ro2:ro2 + 64, ro2:ro2 + 64], gcol,
                         ALU.mult, -1.0, ALU.mult)
                    E.mm(pA[ro:ro + 64, mtl * 64:(mtl + 1) * 64],
                         onesf[ro2:ro2 + 64, 0:64], Dg2[ro2:ro2 + 64, :], True, True)
                if A1_SUB2 < 3:
                    continue
                E.tt(Qch[:], qs3(c0), Ech[:, 0:256], ALU.mult)
                E.tt(Cbch[:], ks3(c0), Einv[:, 0:256], ALU.mult)
                E.tt(kgb[:], ks3(c0), pA[:, 0:256], ALU.mult)
                E.tt(ap3(Bch, 0, 128, 0, [[128, 4], [2, 64]]), ks3(c0), Epch[:, 0:256], ALU.mult)
                E.tt(ap3(Bch, 0, 128, 1, [[128, 4], [2, 64]]), ks3(c0), Ech[:, 0:256], ALU.mult)
                E.tt(ap3(Cmch, 0, 128, 0, [[128, 4], [2, 64]]), kgb[:], Epinv[:, 0:256], ALU.mult)
                E.tt(ap3(Cmch, 0, 128, 1, [[128, 4], [2, 64]]), kgb[:], Einv[:, 0:256], ALU.mult)
                for hl in range(8 if A1_SUB2 >= 4 else 0):
                    mtl, ro = hl // 2, 64 * (hl % 2)
                    Bh = Bch[ro:ro + 64, mtl * 128:(mtl + 1) * 128]
                    Cmh = Cmch[ro:ro + 64, mtl * 128:(mtl + 1) * 128]
                    Cbh = Cbch[ro:ro + 64, mtl * 64:(mtl + 1) * 64]
                    Qh = Qch[ro:ro + 64, mtl * 64:(mtl + 1) * 64]
                    Vh = vTs[ro2:ro2 + 64, (ch // 2) * 512 + hl * 64:(ch // 2) * 512 + (hl + 1) * 64]
                    Sfh = Sf[mtl][ro:ro + 64, :]
                    Sbh = Sb[mtl][ro:ro + 64, :]
                    E.mm(pM[:, 0:128], Cmh, Bh, True, True)
                    E.tt(M1Tb[:], pM[:, 0:128], mSU[:], ALU.mult)
                    E.mm(pM[:, 128:256], Bh, Cmh, True, True)
                    E.tt(M1b[:], pM[:, 128:256], mSL[:], ALU.mult)
                    E.mm(pX[:, 0:64], Cmh, Qh, True, True)
                    E.mm(pX[:, 64:128], Bh, Cbh, True, True)
                    E.mm(pX[:, 128:192], Bh, Sbh, True, True)
                    E.tt(Xb[:], pX[:, 0:192], mZY[:], ALU.mult)
                    if A1_SUB2 < 5:
                        continue
                    E.copy(Wb[:], Xb[:, 64:192])
                    for k in range(7):
                        PTk = M1Tb if k == 0 else PTp[k % 2]
                        Pk = M1b if k == 0 else Pp[k % 2]
                        E.mm(pC[:, 0:128], PTk[:], Wb[:], True, True)
                        E.tt(Wb[:], pC[:, 0:128], Wb[:], ALU.add)
                        if k < 6:
                            E.mm(pC[:, 128:256], PTk[:], Pk[:], True, True)
                            E.copy(Pp[(k + 1) % 2][:], pC[:, 128:256], eng="scalar")
                            E.mm(pC[:, 256:384], Pk[:], PTk[:], True, True)
                            E.copy(PTp[(k + 1) % 2][:], pC[:, 256:384])
                    if A1_SUB2 < 6:
                        continue
                    E.trans(pT[:, 0:64], Cmh, idb[ro:ro + 64, ro:ro + 64])
                    E.copy(CmTb[:], pT[:, 0:64])
                    E.trans(pT[ro2:ro2 + 64, 64:128], Cbh, idb[ro:ro + 64, ro:ro + 64])
                    E.copy(CbTb[ro2:ro2 + 64, :], pT[ro2:ro2 + 64, 64:128], eng="scalar")
                    E.mm(pC[ro2:ro2 + 64, 128:192], Wb[:, 0:64], CmTb[:],
                         True, True, tile_position=(0, ro2))
                    E.tt(CbPTb[ro2:ro2 + 64, :], pC[ro2:ro2 + 64, 128:192],
                         CbTb[ro2:ro2 + 64, :], ALU.add)
                    if A1_SUB2 < 7:
                        continue
                    E.mm(pD[ro:ro + 64, 0:64], CmTb[:], Wb[:, 64:128],
                         True, False, tile_position=(0, ro))
                    E.mm(pD[ro:ro + 64, 0:64], CbPTb[ro2:ro2 + 64, :], Vh,
                         False, True, tile_position=(ro2, ro))
                    if ch >= CH0:
                        E.mm(pE[ro2:ro2 + 64, 0:64], Wb[:, 0:64], Xb[:, 0:64],
                             True, False, tile_position=(0, ro2))
                        E.mm(pE[ro2:ro2 + 64, 0:64], Cbh, Qh, False, True,
                             tile_position=(ro, ro2))
                        E.tt(aTb[ro2:ro2 + 64, :], pE[ro2:ro2 + 64, 0:64],
                             mTri[ro2:ro2 + 64, :], ALU.mult)
                        E.mm(pF[ro:ro + 64, 64:128], Vh, aTb[ro2:ro2 + 64, :],
                             True, True, tile_position=(ro2, ro))
                        E.mm(pF[ro:ro + 64, 0:64], Wb[:, 64:128], Xb[:, 0:64],
                             True, False, tile_position=(0, ro))
                        E.mm(pF[ro:ro + 64, 0:64], Sbh, Qh, False, True,
                             tile_position=(ro, ro))
                        E.copy(ot1[ro:ro + 64, :], pF[ro:ro + 64, 64:128], eng="scalar")
                        gmt = half * 4 + mtl
                        E.tt(OTall[ro:ro + 64, gmt * OW + (ch - CH0) * 64:gmt * OW + (ch - CH0 + 1) * 64],
                             pF[ro:ro + 64, 0:64], ot1[ro:ro + 64, :], ALU.add)
                    E.tt(st1[ro:ro + 64, :], pD[ro:ro + 64, 0:64], Sfh, ALU.add)
                    E.ts(Sfh, st1[ro:ro + 64, :],
                         Ech[ro:ro + 64, mtl * 64 + 63:mtl * 64 + 64], ALU.mult)
                    E.copy(Sbh, Sfh, eng="scalar")
                    if A1_DBG and half == 0 and ch == A1_DBG_CH and hl == 0:
                        self.dbg_dump(fch, "dbg_fch")
                        self.dbg_dump(Lch, "dbg_Lch")
                        self.dbg_dump(Bch, "dbg_Bch")
                        self.dbg_dump(Cmch, "dbg_Cmch")
                        self.dbg_dump(Qch, "dbg_Qch")
                        self.dbg_dump(Cbch, "dbg_Cbch")
                        self.dbg_dump(M1Tb, "dbg_M1T")
                        self.dbg_dump(Xb, "dbg_Xb")
                        self.dbg_dump(Wb, "dbg_Wb")
                        self.dbg_dump(Sf[0], "dbg_Sf0")
                        self.dbg_dump(aTb, "dbg_aTb")
                        self.dbg_dump(OTall, "dbg_OT")
            self.end_scope()

        if not A1_EPI:
            self.end_scope()
            return
        self.begin_scope()
        woutb = self.sb("a1wout", [128, NDT * D], BF16)
        wo2b = self.sb("a1wo2", [64, D], BF16)
        wtmp = self.sb("a1wtmp", [128, D], F32)
        lnw = self.sb("a1lnw", [128, 8], F32)
        gout = self.sb("a1gout", [128, GW], F32)
        gsq = self.sb("a1gsq", [128, GW], F32)
        ybf = self.sb("a1ybf", [128, GW], BF16)
        rows = self.sb("a1rows", [1, 1280], F32)
        rrep = self.sb("a1rrep", [1, 256], F32)
        CH0 = A1_CH0
        E.dma("gpsimd", wo2b[0:64, :], dn["Wo2"][:])
        for mt in range(NDT):
            E.dma("sync", lnw[:, mt:mt + 1],
                  bass.AP(dn["ln_w"], mt * 128, [[1, 128], [1, 1]]))
        for kt in range(NDT):
            E.dma("sync", wtmp[:], dn["Wout"][kt * 128:(kt + 1) * 128, :])
            E.ts(woutb[:, kt * D:(kt + 1) * D], wtmp[:], lnw[:, kt:kt + 1], ALU.mult)
        if A1_DBG:
            self.dbg_dump(OTall, "dbg_OTfull")
            self.dbg_dump(y2b, "dbg_y2b")
        for ch in range(CH0, A1_NCH):
            t0 = max(ch * 64, 1365)
            cnt = (ch + 1) * 64 - t0
            i0 = 3 * (t0 - 1365) + 1
            ocol = (ch - CH0) * 64 + (t0 - ch * 64)
            for mt in range(NDT):
                E.mm(pA[:, mt * 64:mt * 64 + cnt], wo2b[0:64, mt * 128:(mt + 1) * 128],
                     ap3(y2b, 0, 64, i0, [[3, cnt]]), True, True)
            E.act(gsq[:, 0:256], pA[:, 0:256], AF.Sigmoid)
            E.act(gsq[:, 256:512], pA[:, 256:512], AF.Sigmoid)
            if A1_DBG and ch == 21:
                dbgG = self.sb("dbgG", [128, 512], F32)
                E.copy(dbgG[:], gsq[:])
                self.dbg_dump(dbgG, "dbg_gate")
            E.tt(gout[:, 0:8 * cnt],
                 ap3(OTall, 0, 128, ocol, [[OW, 8], [1, cnt]]),
                 ap3(gsq, 0, 128, 0, [[64, 8], [1, cnt]]), ALU.mult)
            E.act(gsq[:, 0:8 * cnt], gout[:, 0:8 * cnt], AF.Square)
            E.mm(pM[0:1, 0:8 * cnt], onesf[:, 0:1], gout[:, 0:8 * cnt], True, True)
            E.mm(pX[0:1, 0:8 * cnt], onesf[:, 0:1], gsq[:, 0:8 * cnt], True, True)
            E.copy(rows[0:1, 0:8 * cnt], pM[0:1, 0:8 * cnt])
            E.copy(rows[0:1, 512:512 + 8 * cnt], pX[0:1, 0:8 * cnt], eng="scalar")
            mu = rows[0:1, 1024:1024 + cnt]; ms = rows[0:1, 1088:1088 + cnt]
            vr = rows[0:1, 1152:1152 + cnt]; rr = rows[0:1, 1216:1216 + cnt]
            E.red(mu, ap3(rows, 0, 1, 0, [[1, cnt], [cnt, 8]]), ALU.add)
            E.red(ms, ap3(rows, 0, 1, 512, [[1, cnt], [cnt, 8]]), ALU.add)
            E.ts(mu, mu, 1.0 / D, ALU.mult)
            E.ts(ms, ms, 1.0 / D, ALU.mult)
            E.tt(vr, mu, mu, ALU.mult)
            E.tt(vr, ms, vr, ALU.subtract)
            E.act(vr, vr, AF.Sqrt, bias=self.fc[0:1, 0:1])
            E.recip(rr, vr)
            E.tt(mu, mu, rr, ALU.mult)
            E.copy(rrep[0:1, 0:4 * cnt], ap3(rows, 0, 1, 1216, [[0, 4], [1, cnt]]))
            E.mm(pC[:, 0:4 * cnt], onesf[0:1, :], rrep[0:1, 0:4 * cnt], True, True)
            E.copy(rrep[0:1, 0:4 * cnt], ap3(rows, 0, 1, 1024, [[0, 4], [1, cnt]]))
            E.mm(pD[:, 0:4 * cnt], onesf[0:1, :], rrep[0:1, 0:4 * cnt], True, True)
            for pp in range(2):
                hsl = slice(pp * 4 * cnt, (pp + 1) * 4 * cnt)
                psl = slice(0, 4 * cnt)
                E.tt(gout[:, hsl], gout[:, hsl], pC[:, psl], ALU.mult)
                E.tt(ybf[:, hsl], gout[:, hsl], pD[:, psl], ALU.subtract)
            for mt in range(NDT):
                for kt in range(NDT):
                    E.mm(pF[:, mt * 64:mt * 64 + cnt],
                         woutb[:, kt * D + mt * 128:kt * D + (mt + 1) * 128],
                         ybf[:, kt * cnt:(kt + 1) * cnt], kt == 0, kt == NDT - 1)
            if A1_DBG and ch == 21:
                self.dbg_dump(gsq, "dbg_gsq1")
                self.dbg_dump(gout, "dbg_gout")
                self.dbg_dump(ybf, "dbg_ybf")
                self.dbg_dump(rows, "dbg_rows")
            for mt in range(NDT):
                xap = ap3(self.xr[mt], 0, 128, i0, [[3, cnt]])
                E.tt(xap, xap, pF[:, mt * 64:mt * 64 + cnt], ALU.add)
        self.end_scope()
        self.end_scope()


_PROGS = {}


def get_prog(sections, dump=()):
    key = (tuple(sections), tuple(dump))
    if key not in _PROGS:
        p = Prog(sections, dump)
        p.load_x()
        for s in sections:
            p.E.rotate_fresh()
            p.sec_ln()
            getattr(p, "sec_" + s)()
        p.store_x()
        for i, dn in enumerate(dump):
            tens = {t.name: t for t in (p.xr + p.xln)}[dn]
            dt = F32 if dn.startswith("xr") else BF16
            od = p.nc.dram_tensor(f"dump{i}", list(tens.shape), dt, kind="ExternalOutput")
            p.E.dma("sync", od[:], tens[:])
        p.E.barrier()
        p.E.finalize()
        _PROGS[key] = p
    return _PROGS[key]


LAUNCH_PLAN = [["moe"], ["attn0"], ["moe"], ["ffn1"], ["attn1"], ["ffn1"]]


def kernel(**inputs):
    C = make_consts()
    x = np.ascontiguousarray(inputs["L0_attn_x"]).astype(np.float32)
    shards = [x[i] for i in range(NCORES)]
    base = {"c_" + n: np.ascontiguousarray(v) for n, v in C.items()}
    for n, v in inputs.items():
        if n != "L0_attn_x":
            base[n] = np.ascontiguousarray(v)
    plan = [sum(LAUNCH_PLAN, [])] if FUSED else LAUNCH_PLAN
    cur = shards
    for sections in plan:
        p = get_prog(sections)
        in_maps = []
        for ci in range(NCORES):
            m = {n: base[n] for n in p.inputs if n != "x_in"}
            m["x_in"] = cur[ci]
            in_maps.append(m)
        res = run_bass_kernel_spmd(p.nc, in_maps, core_ids=list(range(NCORES)))
        cur = [res.results[ci]["x_out"] for ci in range(NCORES)]
    return np.stack(cur, 0)
